# version 24
# baseline (speedup 1.0000x reference)
import math
from contextlib import ExitStack

import numpy as np
import concourse.bass as bass
import concourse.mybir as mybir
from concourse.bass_utils import run_bass_kernel_spmd

F32 = mybir.dt.float32
BF16 = mybir.dt.bfloat16
AF = mybir.ActivationFunctionType
ALU = mybir.AluOpType
AX = mybir.AxisListType

S = 4096
D = 1024
H = 16
DH = 64
G = 256
ND = 8
NP = 8
CH = 128
NCH = S // CH
NTK = S // 128
VSLOT = 66
EXP_BIAS = -10.0
EPS = 1e-5
NCORES = 8

ARENA_COLS = 49152


class Sem:
    def __init__(self, ctx, nc, name):
        self.h = ctx.enter_context(nc.semaphore(name))
        self.v = 0


class Arena:

    def __init__(self, h):
        self.h = h
        self.top = 0

    def f32(self, rows, cols):
        o = self.top
        self.top += cols
        assert self.top <= ARENA_COLS, f"SBUF arena overflow: {self.top}"
        return self.h[0:rows, o:o + cols]

    def bf16(self, rows, cols):
        o = self.top
        self.top += (cols + 1) // 2
        assert self.top <= ARENA_COLS, f"SBUF arena overflow: {self.top}"
        return self.h[0:128, o:o + (cols + 1) // 2].bitcast(BF16)[0:rows, 0:cols]

    def mark(self):
        return self.top

    def reset(self, m):
        self.top = m


class Prog:

    KEYS = ("t", "v", "s", "g", "y")
    DMA_DEPTH = 8

    def __init__(self):
        self.ops = {k: [] for k in self.KEYS}
        self.dmacnt = {}

    def set_dma_sems(self, sems):
        self.dma_hs = {id(s.h) for s in sems}
        self.dma_h2 = {s.h for s in sems}

    def __init_sems__(self, esem):
        self.esem = esem
        self.bneed = set()

    def emit(self, key, fn, wait=(), inc=(), inc16=(), wait_raw=(), noauto=False):
        waits = tuple((s.h, s.v) for s in wait) + tuple(wait_raw)
        for s in wait:
            if s in self.esem.values():
                self.bneed.add(s)
        own = self.esem.get(key)
        auto = key in ("v", "s", "g") and not noauto
        if own is not None and own in self.bneed:
            if not waits:
                waits = ((own.h, own.v),)
            self.bneed.discard(own)
        step = 16 if key in ("g", "y") else 1
        pre = tuple((s.h, s.v) for s in inc)
        incs = tuple([(s.h, step) for s in inc] + [(s.h, 16) for s in inc16])
        if auto and not incs:
            incs = ((own.h, step),)
            own.v += step
        if incs and not inc16:
            dma_hs = getattr(self, "dma_hs", set())
            if not any(id(h) not in dma_hs for h, v in waits):
                sw = (own.h, own.v - (step if (auto or own in inc) else 0)) \
                    if own is not None else (pre[0] if pre else (incs[0][0], 0))
                waits = waits + (sw,)
        for s in inc:
            s.v += step
        for s in inc16:
            s.v += 16

        def cl(eng, fn=fn, waits=waits, incs=incs):
            for h, v in waits:
                eng.wait_ge(h, v)
            inst = fn(eng)
            for h, n in incs:
                inst.then_inc(h, n)

        self.ops[key].append(cl)


class DmaQ:

    def __init__(self, P, key, sems):
        self.P, self.key, self.sems, self.i = P, key, sems, 0

    def batch(self, pairs, wait=(), wait_raw=()):
        s = self.sems[self.i % len(self.sems)]
        self.i += 1
        pairs = list(pairs)
        for n, (dst, srcap) in enumerate(pairs):
            self.P.emit(self.key, mk_dma(dst, srcap),
                        wait=(tuple(wait) + (s,)) if n == 0 else (),
                        wait_raw=wait_raw if n == 0 else (), inc16=[s])
        return (s.h, s.v)



def mk_nop():
    return lambda e: e.nop()


def mk_dma(dst, src):
    return lambda e: e.dma_start(out=dst, in_=src)


def mk_mm(o, a, b, st, sp, tp=None, skip=False):
    if tp is None:
        return lambda T: T.matmul(o, a, b, start=st, stop=sp, skip_group_check=skip)
    return lambda T: T.matmul(o, a, b, start=st, stop=sp, tile_position=tp,
                              skip_group_check=skip)


def mk_trans(o, i, ident):
    return lambda T: T.transpose(o, i, ident)


def mk_act(o, i, func, scale=None, bias=None):
    kw = {}
    if scale is not None:
        kw["scale"] = scale
    if bias is not None:
        kw["bias"] = bias
    return lambda A: A.activation(out=o, in_=i, func=func, **kw)


def mk_copy(o, i):
    return lambda V: V.tensor_copy(out=o, in_=i)


def mk_tt(o, a, b, opname):
    op = getattr(ALU, opname)
    return lambda V: V.tensor_tensor(out=o, in0=a, in1=b, op=op)


def mk_ts(o, a, s1, opname):
    op = getattr(ALU, opname)
    return lambda V: V.tensor_scalar(out=o, in0=a, scalar1=s1, scalar2=None, op0=op)


def mk_ts2(o, a, s1, op0name, s2, op1name):
    op0 = getattr(ALU, op0name)
    op1 = getattr(ALU, op1name)
    return lambda V: V.tensor_scalar(out=o, in0=a, scalar1=s1, scalar2=s2,
                                     op0=op0, op1=op1)


def mk_stt(o, a, s1, b, op0name, op1name):
    op0 = getattr(ALU, op0name)
    op1 = getattr(ALU, op1name)
    return lambda V: V.scalar_tensor_tensor(out=o, in0=a, scalar=s1, in1=b,
                                            op0=op0, op1=op1)


def mk_red(o, i, opname):
    op = getattr(ALU, opname)
    return lambda V: V.tensor_reduce(out=o, in_=i, axis=AX.X, op=op)


def mk_recip(o, i):
    return lambda V: V.reciprocal(out=o, in_=i)


def mk_max8(o, i):
    return lambda V: V.max(out=o, in_=i)


def mk_mr(o, m8, vals, imm):
    return lambda V: V.match_replace(out=o, in_to_replace=m8, in_values=vals,
                                     imm_value=imm)


def mk_memset(ap, val):
    return lambda V: V.memset(ap, val)


def mk_coll(ins, outs, sem):
    def f(GP):
        i = GP.collective_compute(
            "AllReduce", ALU.add, replica_groups=[list(range(NCORES))],
            ins=[ins], outs=[outs])
        i.then_inc(sem)
        return _NoInc(i)
    return f


class _NoInc:
    def __init__(self, i):
        self.i = i

    def then_inc(self, h, n):
        raise AssertionError("collective inc is managed internally")



def build_nc(debug_out=(), phases=("p01", "p2", "p3", "cc", "fb")):
    nc = bass.Bass("TRN2", target_bir_lowering=False, debug=False)

    def din(name, shape, dt=F32):
        return nc.dram_tensor(name, list(shape), dt, kind="ExternalInput").ap()

    xT = din("xT", (128, ND, S))
    pet = din("pet", (128, ND, S))
    wq = din("wq", (D, D))
    wk = din("wk", (D, D))
    wv = din("wv", (D, D))
    wo = din("wo", (D, D))
    bq = din("bq", (128, ND))
    bk = din("bk", (128, ND))
    bvr = din("bvr", (1, D))
    bo = din("bo", (128, ND))
    gam2 = din("gam2", (128, ND))
    bet2 = din("bet2", (128, ND))
    ffw = din("ffw", (D, D))
    ffb = din("ffb", (128, ND))
    cmso = din("cmso", (D, G))
    cmsi = din("cmsi", (D, G))
    cbso = din("cbso", (128, 2))
    cbsi = din("cbsi", (128, 2))
    gso = din("gso", (128, 2))
    bso = din("bso", (128, 2))
    gsi = din("gsi", (128, 2))
    bsi = din("bsi", (128, 2))
    lwso = din("lwso", (G, D))
    lwsi = din("lwsi", (G, D))
    lbso = din("lbso", (1, D))
    lbsi = din("lbsi", (1, D))
    identf = din("identf", (128, 128))
    ctab = din("ctab", (128, 16))
    onesrow = din("onesrow", (1, 128))

    yout = nc.dram_tensor("yout", [128, ND, S], F32, kind="ExternalOutput").ap()

    def scratch(name, shape, dt):
        kind = "ExternalOutput" if name in debug_out else "Internal"
        return nc.dram_tensor(name, list(shape), dt, kind=kind).ap()

    hT_d = scratch("hT_d", (128, ND, S), F32)
    qT_d = scratch("qT_d", (128, ND, S), BF16)
    hbf_d = scratch("hbf_d", (128, ND, S), BF16)
    on_d = scratch("on_d", (NCH, 128, D), BF16)
    x1_d = scratch("x1_d", (128, ND, S), F32)
    uT_d = scratch("uT_d", (128, ND, S), BF16)
    xc_d = scratch("xc_d", (128, 4, S), F32)
    fu_d = scratch("fu_d", (128, ND, S), BF16)
    cc_i = nc.dram_tensor("cc_i", [128, 8], F32, kind="Internal").ap()
    cc_o = nc.dram_tensor("cc_o", [128, 8], F32, kind="Internal").ap()

    P = Prog()
    est = ExitStack()
    with est:
        arena_t = est.enter_context(nc.sbuf_tensor("arena", [128, ARENA_COLS], F32))
        AR = Arena(arena_t)

        PA = est.enter_context(nc.psum_tensor("PA", [128, 1024], F32))
        PB = est.enter_context(nc.psum_tensor("PB", [128, 1024], F32))
        PC = est.enter_context(nc.psum_tensor("PC", [128, 1024], F32))
        PD = est.enter_context(nc.psum_tensor("PD", [128, 512], F32))
        PE2 = est.enter_context(nc.psum_tensor("PE2", [128, 512], F32))

        lqs = [Sem(est, nc, f"lq{i}") for i in range(4)]
        sqs = [Sem(est, nc, f"sq{i}") for i in range(4)]
        hqs = [Sem(est, nc, f"hq{i}") for i in range(2)]
        LQ = DmaQ(P, "y", lqs)
        SQ = DmaQ(P, "g", sqs)
        HQ = DmaQ(P, "y", hqs)
        LQA = tuple(lqs)
        SQA = tuple(sqs)
        HQA = tuple(hqs)
        pe = Sem(est, nc, "pe")
        ve = Sem(est, nc, "ve")
        ac = Sem(est, nc, "ac")
        gp = Sem(est, nc, "gp")
        ccs = Sem(est, nc, "ccs")
        P.__init_sems__({"t": pe, "v": ve, "s": ac, "g": gp})
        P.set_dma_sems(lqs + sqs + hqs)

        def barrier():
            P.emit("t", mk_nop(), wait=[pe], inc=[pe])
            P.emit("v", mk_nop())
            P.emit("s", mk_nop())
            P.emit("g", mk_nop())
            allsems = [pe, ve, ac, gp] + list(LQA) + list(SQA) + list(HQA)
            for k in P.KEYS:
                P.emit(k, mk_nop(), wait=allsems)

        cons = AR.f32(128, 16)
        ones_c = AR.bf16(128, 2)
        ones_r = AR.f32(1, 128)
        ones_rb = AR.bf16(1, 128)
        ident_b = AR.bf16(128, 128)
        bq_s = AR.f32(128, ND)
        bk_s = AR.f32(128, ND)
        bo_s = AR.f32(128, ND)
        g2_s = AR.f32(128, ND)
        b2_s = AR.f32(128, ND)
        ffb_s = AR.f32(128, ND)
        cb_s = AR.f32(128, 4)
        gg_s = AR.f32(128, 4)
        bb_s = AR.f32(128, 4)
        lb_b = AR.bf16(1, 2 * D)
        vbb = AR.f32(128, D)
        bnacc = AR.f32(128, 8)
        bnp_sc = AR.f32(128, 4)
        bnp_bi = AR.f32(128, 4)

        CONSTS = {0: 1.0, 1: 0.125, 2: EXP_BIAS, 3: 1.0 / D, 4: -0.5, 5: EPS,
                  6: 1.0 / 32768.0, 7: 0.0, 8: -1.0}
        for col, val in CONSTS.items():
            nc.const_aps.register(F32, val, cons[:, col:col + 1])

        M_PHASE = AR.mark()
        v_sb = AR.bf16(128, NTK * H * VSLOT)
        k_sb = [AR.bf16(128, S) for _ in range(2)]
        wpq = [AR.bf16(128, ND * 128) for _ in range(2)]
        wpk = [AR.bf16(128, ND * 128) for _ in range(2)]
        wpv = [AR.bf16(128, ND * 128) for _ in range(2)]
        wtmpq = [AR.f32(128, 128) for _ in range(2)]
        qstg = [AR.bf16(128, 256) for _ in range(2)]
        M_V = AR.mark()

        NW = S // 256
        pqk = PD[:, 256:512]
        pvv = PE2[:, 256:384]
        qstb = {}
        qkv_pe_done = {}
        wcast_done = {}
        wst_hist = []
        qk_evac = {}
        v_evh = {}
        hw_pe = []

        def stage_wpair(p):
            first_war = ([(pe.h, qkv_pe_done[p - 2])] if p - 2 in qkv_pe_done
                         else [])
            for mi, (wt, dstl) in enumerate(((wq, wpq), (wk, wpk), (wv, wpv))):
                for k in range(ND):
                    i = len(wst_hist)
                    wr = [wst_hist[i - 2]] if i >= 2 else []
                    bw = HQ.batch([(wtmpq[i % 2][:, :],
                                    wt[k * 128:(k + 1) * 128, p * 128:(p + 1) * 128])],
                                  wait_raw=wr)
                    first = mi == 0 and k == 0
                    P.emit("v", mk_copy(dstl[p % 2][:, k * 128:(k + 1) * 128],
                                        wtmpq[i % 2][:, :]),
                           wait=[ve],
                           wait_raw=[bw] + (first_war if first else []))
                    wst_hist.append((ve.h, ve.v))
            wcast_done[p] = (ve.h, ve.v)

        def qkv_proj(p, w, hap, which, slot, chain):
            T = w * 256
            if chain:
                P.emit("t", mk_nop(), wait_raw=list(chain))
            wsrc = wpk if which == "k" else wpq
            for k in range(ND):
                P.emit("t", mk_mm(slot, wsrc[p % 2][:, k * 128:(k + 1) * 128],
                                  hap(k), k == 0, k == ND - 1),
                       inc=([pe] if k == ND - 1 else []))
            if which == "k":
                P.emit("v", mk_ts(k_sb[p % 2][:, T:T + 256], slot,
                                  bk_s[:, p:p + 1], "add"), wait=[ve],
                       wait_raw=[(pe.h, pe.v)])
            else:
                old_q = [qstb[p][w - 2]] if w >= 2 else []
                P.emit("v", mk_ts(qstg[w % 2][:, :], slot, bq_s[:, p:p + 1], "add"),
                       wait=[ve], wait_raw=[(pe.h, pe.v)] + old_q)
            P.emit("v", mk_nop())
            qk_evac[(p, which, w)] = ve.v
            if which == "q":
                qstb.setdefault(p, []).append(
                    SQ.batch([(qT_d[:, p, T:T + 256], qstg[w % 2][:, :])], wait=[ve]))

        def qkv_v_tile(p, tk, hap_t, slot, chain):
            evh = v_evh.setdefault(p, [])
            if chain:
                P.emit("t", mk_nop(), wait_raw=list(chain))
            for k in range(ND):
                P.emit("t", mk_mm(slot, hap_t(k),
                                  wpv[p % 2][:, k * 128:(k + 1) * 128],
                                  k == 0, k == ND - 1),
                       inc=([pe] if k == ND - 1 else []))
            P.emit("v", mk_nop(), wait=[ve], wait_raw=[(pe.h, pe.v)])
            for h01 in range(2):
                hidx = 2 * p + h01
                base = (tk * H + hidx) * VSLOT
                P.emit("v", mk_tt(v_sb[:, base:base + 64],
                                  slot[:, h01 * 64:(h01 + 1) * 64],
                                  vbb[:, hidx * 64:(hidx + 1) * 64], "add"))
            P.emit("v", mk_nop())
            evh.append(ve.v)

        def emitP0():
            loads = [
                (cons, ctab[:, :]), (bq_s, bq[:, :]), (bk_s, bk[:, :]),
                (bo_s, bo[:, :]), (g2_s, gam2[:, :]), (b2_s, bet2[:, :]),
                (ffb_s, ffb[:, :]),
                (cb_s[:, 0:2], cbso[:, :]), (cb_s[:, 2:4], cbsi[:, :]),
                (gg_s[:, 0:2], gso[:, :]), (gg_s[:, 2:4], gsi[:, :]),
                (bb_s[:, 0:2], bso[:, :]), (bb_s[:, 2:4], bsi[:, :]),
                (ones_r[0:1, :], onesrow[:, :]),
            ]
            tmp = AR.f32(128, 128)
            tmpr = AR.f32(1, 2 * D)
            tmpv = AR.f32(1, D)
            loads += [(tmp[:, :], identf[:, :]), (tmpr[0:1, 0:D], lbso[:, :]),
                      (tmpr[0:1, D:2 * D], lbsi[:, :]), (tmpv[0:1, :], bvr[:, :])]
            b0 = LQ.batch(loads)
            P.emit("v", mk_copy(ident_b[:, :], tmp[:, :]), wait_raw=[b0])
            P.emit("v", mk_copy(lb_b[0:1, :], tmpr[0:1, :]))
            P.emit("v", mk_copy(ones_rb[0:1, :], ones_r[0:1, :]))
            P.emit("v", mk_copy(ones_c[:, 0:1], cons[:, 0:1]))
            P.emit("v", mk_memset(bnacc[:, :], 0.0))
            for half in range(2):
                P.emit("t", mk_mm(PA[:, half * 512:(half + 1) * 512],
                                  ones_r[0:1, :], tmpv[0:1, half * 512:(half + 1) * 512],
                                  True, True), wait=[ve], wait_raw=[b0], inc=[pe])
            P.emit("v", mk_copy(vbb[:, :], PA[:, 0:1024]), wait=[pe])
            P.emit("t", mk_nop(), wait=[ve])

        emitP0()
        barrier()

        def emitP1():
            CB = 512
            NCB = S // CB
            NT = CB // 128
            stage_wpair(0)
            stage_wpair(1)
            x_c = AR.f32(128, ND * CB)
            pe_c = AR.f32(128, ND * CB)
            xbf2 = [AR.bf16(128, ND * CB) for _ in range(2)]
            st_c = AR.f32(1, 3 * CB)
            rst_c = AR.f32(1, CB)
            vtpl = AR.bf16(128, H * VSLOT)

            P.emit("v", mk_memset(vtpl[:, :], 0.0))
            P.emit("v", mk_nop(), wait=[ve])
            for hh in range(H):
                P.emit("v", mk_memset(vtpl[:, hh * VSLOT + 64:hh * VSLOT + 65], 1.0))
            P.emit("v", mk_nop(), wait=[ve])
            for tk in range(NTK):
                P.emit("v", mk_copy(v_sb[:, tk * H * VSLOT:(tk + 1) * H * VSLOT],
                                    vtpl[:, :]))
            P.emit("v", mk_nop())

            psum_a = PA[0:1, 0:CB]
            psum_b = PA[0:1, 512:512 + CB]
            pbc = PC[:, 0:2 * CB]

            def ld_p1(c):
                t0, t1 = c * CB, (c + 1) * CB
                prs = ([(x_c[:, k * CB:(k + 1) * CB], xT[:, k, t0:t1]) for k in range(ND)]
                       + [(pe_c[:, k * CB:(k + 1) * CB], pet[:, k, t0:t1]) for k in range(ND)])
                return tuple(LQ.batch(prs[i::4], wait=[ve]) for i in range(4))

            bc = ld_p1(0)
            wv_pe = {}
            bhb_hist = {}
            for c in range(NCB):
                t0, t1 = c * CB, (c + 1) * CB
                xbf = xbf2[c % 2]
                hbf = xbf
                war = ([(pe.h, wv_pe[c - 2])] if c >= 2 else []) \
                    + ([bhb_hist[c - 2]] if c >= 2 else [])
                P.emit("v", mk_nop(), wait_raw=list(bc) + war)
                for k in range(ND):
                    P.emit("v", mk_copy(xbf[:, k * CB:(k + 1) * CB], x_c[:, k * CB:(k + 1) * CB]))
                P.emit("v", mk_nop())
                P.emit("t", mk_nop(), wait=[ve])
                for k in range(ND):
                    P.emit("t", mk_mm(psum_a, ones_c[:, 0:1], xbf[:, k * CB:(k + 1) * CB],
                                      k == 0, k == ND - 1), inc=([pe] if k == ND - 1 else []))
                P.emit("v", mk_nop(), wait=[pe])
                for k in range(ND):
                    P.emit("v", mk_tt(xbf[:, k * CB:(k + 1) * CB], xbf[:, k * CB:(k + 1) * CB],
                                      xbf[:, k * CB:(k + 1) * CB], "mult"))
                P.emit("v", mk_nop())
                P.emit("t", mk_nop(), wait=[ve])
                for k in range(ND):
                    P.emit("t", mk_mm(psum_b, ones_c[:, 0:1], xbf[:, k * CB:(k + 1) * CB],
                                      k == 0, k == ND - 1), inc=([pe] if k == ND - 1 else []))
                P.emit("v", mk_ts(st_c[0:1, 0:CB], psum_a, 1.0 / D, "mult"), wait=[pe])
                P.emit("v", mk_ts(st_c[0:1, CB:2 * CB], psum_b, 1.0 / D, "mult"))
                P.emit("v", mk_tt(rst_c[0:1, 0:CB], st_c[0:1, 0:CB], st_c[0:1, 0:CB], "mult"),
                       wait=[ve])
                P.emit("v", mk_tt(st_c[0:1, 2 * CB:3 * CB], st_c[0:1, CB:2 * CB],
                                  rst_c[0:1, 0:CB], "subtract"), wait=[ve])
                P.emit("v", mk_ts(st_c[0:1, 2 * CB:3 * CB], st_c[0:1, 2 * CB:3 * CB], EPS, "add"),
                       wait=[ve])
                P.emit("s", mk_act(rst_c[0:1, 0:CB], st_c[0:1, 2 * CB:3 * CB], AF.Ln),
                       wait=[ve])
                P.emit("s", mk_act(st_c[0:1, 2 * CB:3 * CB], rst_c[0:1, 0:CB], AF.Exp, scale=-0.5),
                       wait=[ac])
                P.emit("v", mk_tt(st_c[0:1, CB:2 * CB], st_c[0:1, 0:CB],
                                  st_c[0:1, 2 * CB:3 * CB], "mult"), wait=[ac])
                P.emit("t", mk_mm(pbc[:, 0:CB], ones_r[0:1, :], st_c[0:1, 2 * CB:3 * CB], True, True),
                       wait=[ve])
                P.emit("t", mk_mm(pbc[:, CB:2 * CB], ones_r[0:1, :], st_c[0:1, CB:2 * CB], True, True),
                       inc=[pe])
                P.emit("v", mk_nop(), wait=[pe])
                for k in range(ND):
                    P.emit("v", mk_tt(x_c[:, k * CB:(k + 1) * CB], x_c[:, k * CB:(k + 1) * CB],
                                      pbc[:, 0:CB], "mult"))
                P.emit("v", mk_nop(), wait=[ve])
                for k in range(ND):
                    P.emit("v", mk_tt(x_c[:, k * CB:(k + 1) * CB], x_c[:, k * CB:(k + 1) * CB],
                                      pbc[:, CB:2 * CB], "subtract"))
                P.emit("v", mk_nop(), wait=[ve])
                for k in range(ND):
                    P.emit("v", mk_tt(x_c[:, k * CB:(k + 1) * CB], x_c[:, k * CB:(k + 1) * CB],
                                      pe_c[:, k * CB:(k + 1) * CB], "add"))
                P.emit("v", mk_nop(), wait=[ve])
                for k in range(ND):
                    P.emit("v", mk_copy(hbf[:, k * CB:(k + 1) * CB], x_c[:, k * CB:(k + 1) * CB]))
                P.emit("v", mk_nop())
                hcast_val = ve.v
                bhb = SQ.batch([(hbf_d[:, k, t0:t1], hbf[:, k * CB:(k + 1) * CB])
                                for k in range(ND)], wait=[ve])
                bhb_hist[c] = bhb
                if c + 1 < NCB:
                    ldp = []
                    for k in range(ND):
                        bsk = SQ.batch([(hT_d[:, k, t0:t1], x_c[:, k * CB:(k + 1) * CB])],
                                       wait=([ve] if k == 0 else ()))
                        ldp.append((k, bsk))
                    lb = []
                    for k, bsk in ldp:
                        lb.append(LQ.batch(
                            [(x_c[:, k * CB:(k + 1) * CB], xT[:, k, (c + 1) * CB:(c + 2) * CB]),
                             (pe_c[:, k * CB:(k + 1) * CB], pet[:, k, (c + 1) * CB:(c + 2) * CB])],
                            wait_raw=[bsk]))
                    bc = tuple(lb)
                else:
                    SQ.batch([(hT_d[:, k, t0:t1], x_c[:, k * CB:(k + 1) * CB])
                              for k in range(ND)], wait=[ve])
                P.emit("t", mk_nop(), wait_raw=[(ve.h, hcast_val)])
                for wi in range(CB // 256):
                    w = c * (CB // 256) + wi
                    hap = (lambda wi=wi: lambda k:
                           hbf[:, k * CB + wi * 256:k * CB + wi * 256 + 256])()
                    kchain = ([wcast_done[0]] if w == 0
                              else [(ve.h, qk_evac[(0, "q", w - 1)])])
                    qkv_proj(0, w, hap, "k", pqk, kchain)
                    qkv_proj(0, w, hap, "q", pqk, [(ve.h, qk_evac[(0, "k", w)])])
                    for i2 in range(2):
                        tk = 2 * w + i2
                        hapt = (lambda wi=wi, i2=i2: lambda k:
                                hbf[:, k * CB + wi * 256 + i2 * 128:
                                    k * CB + wi * 256 + (i2 + 1) * 128])()
                        vchain = ([(ve.h, v_evh[0][tk - 1])] if tk >= 1 else [])
                        qkv_v_tile(0, tk, hapt, pvv, vchain)
                wv_pe[c] = pe.v
            qkv_pe_done[0] = pe.v
            AR.reset(M_V)

        if "p01" in phases:
            emitP1()
        barrier()

        def emitP2():
            m2 = AR.mark()
            eT = [AR.bf16(128, 2 * NTK * CH) for _ in range(3)]
            q_p = [AR.bf16(128, CH) for _ in range(4)]
            ostg = [AR.bf16(128, 2 * DH) for _ in range(2)]
            rec = AR.f32(128, 2)
            hwin = [AR.bf16(128, ND * 256) for _ in range(2)]

            pgs = [PA[:, 0:1024], PB[:, 0:1024], PC[:, 0:1024]]
            pos_ = [PD[:, 0:132], PE2[:, 0:132]]
            NGRP = NTK // 4

            hwb = {}
            lwv = [None]
            def hwin_load(g):
                T = (g % NW) * 256
                wr = [(pe.h, hw_pe[g - 2])] if g >= 2 else []
                hwb[g] = HQ.batch([(hwin[g % 2][:, k * 256:(k + 1) * 256],
                                    hbf_d[:, k, T:T + 256]) for k in range(ND)],
                                  wait_raw=wr)

            if NP > 1:
                hwin_load(0)
                if NW > 1:
                    hwin_load(1)
            exp_hist = []
            av_hist = []
            sc_hist = []
            dv_hist = []
            dm_hist = []
            bq_pend = {}

            bq_defer = []

            def issue_bq(gcc):
                jj, cc = gcc // NCH, gcc % NCH
                if len(qstb.get(jj, ())) <= cc // 2:
                    bq_defer.append(gcc)
                    return
                war = [(pe.h, sc_hist[gcc - 4])] if gcc >= 4 else []
                bq_pend[gcc] = LQ.batch(
                    [(q_p[gcc % 4][:, :], qT_d[:, jj, cc * CH:(cc + 1) * CH])],
                    wait_raw=war + [qstb[jj][cc // 2]])

            def flush_bq():
                while bq_defer:
                    jj, cc = bq_defer[0] // NCH, bq_defer[0] % NCH
                    if len(qstb.get(jj, ())) <= cc // 2:
                        break
                    issue_bq(bq_defer.pop(0))

            for gc0 in range(min(3, NP * NCH)):
                issue_bq(gc0)
            for j in range(NP):
                if j + 2 < NP:
                    stage_wpair(j + 2)
                for c in range(NCH + 1):
                    gc = j * NCH + c
                    if c < NCH:
                        t0, t1 = c * CH, (c + 1) * CH
                        kw = ([(ve.h, qk_evac[(j, "k", NW - 1)])] if c == 0 else [])
                        P.emit("t", mk_nop(), wait_raw=[bq_pend[gc]] + kw)
                        if gc + 3 < NP * NCH:
                            issue_bq(gc + 3)
                        et = eT[gc % 3]
                        for m in range(NGRP):
                            gidx = gc * NGRP + m
                            pg = pgs[gidx % 3]
                            if gidx >= 3:
                                P.emit("t", mk_nop(), wait_raw=[(ac.h, exp_hist[gidx - 3])])
                            for q4 in range(4):
                                tk = m * 4 + q4
                                P.emit("t", mk_mm(pg[:, q4 * 128:(q4 + 1) * 128],
                                                  k_sb[j % 2][0:64, tk * 128:(tk + 1) * 128],
                                                  q_p[c % 4][0:64, :], True, True, tp=(0, 0)))
                                P.emit("t", mk_mm(pg[:, 512 + q4 * 128:512 + (q4 + 1) * 128],
                                                  k_sb[j % 2][64:128, tk * 128:(tk + 1) * 128],
                                                  q_p[c % 4][64:128, :], True, True, tp=(64, 0)),
                                       inc=([pe] if q4 == 3 else []))
                            extra = [(pe.h, av_hist[gc - 3])] if (gc >= 3 and m == 0) else []
                            P.emit("s", mk_act(et[:, m * 1024:(m + 1) * 1024], pg, AF.Exp,
                                               scale=0.125, bias=EXP_BIAS),
                                   wait=[pe], wait_raw=extra)
                            exp_hist.append(ac.v)
                            if m == NGRP - 1:
                                sc_hist.append(pe.v)
                    weave = j + 1 < NP and 1 <= c <= 2 * NW
                    if weave:
                        g = j * NW + (c - 1) // 2
                        w = g % NW
                        hv = hwin[g % 2]
                        if c % 2 == 1:
                            P.emit("t", mk_nop(), wait_raw=[hwb[g]])
                        else:
                            kchain = ([wcast_done[j + 1]] if w == 0
                                      else [(ve.h, qk_evac[(j + 1, "q", w - 1)])]) \
                                + ([(ve.h, dv_hist[-1])] if dv_hist else [])
                            qkv_proj(j + 1, w,
                                     (lambda hv=hv: lambda k:
                                      hv[:, k * 256:(k + 1) * 256])(),
                                     "k", pqk, kchain)
                    if c >= 1:
                        ca = c - 1
                        ag = j * NCH + ca
                        po = pos_[ca % 2]
                        eta = eT[ag % 3]
                        if ag >= 2:
                            P.emit("t", mk_nop(), wait_raw=[(ve.h, dv_hist[ag - 2])])
                        P.emit("t", mk_nop(), wait_raw=[(ac.h, exp_hist[ag * NGRP + NGRP - 1])]
                               + ([(ve.h, lwv[0])] if lwv[0] is not None else [])
                               + ([(ve.h, v_evh[j][-1])] if ca == 0 else []))
                        for h01 in range(2):
                            hidx = 2 * j + h01
                            ocol = h01 * VSLOT
                            for tk in range(NTK):
                                ecol = (tk // 4) * 1024 + h01 * 512 + (tk % 4) * 128
                                P.emit("t", mk_mm(po[:, ocol:ocol + 65],
                                                  eta[:, ecol:ecol + 128],
                                                  v_sb[:, (tk * H + hidx) * VSLOT:
                                                       (tk * H + hidx) * VSLOT + 65],
                                                  tk == 0, tk == NTK - 1),
                                       inc=([pe] if (h01 == 1 and tk == NTK - 1) else []))
                        av_hist.append(pe.v)
                    if weave:
                        if c % 2 == 1:
                            for tk in (2 * w, 2 * w + 1):
                                vchain = ([(ve.h, v_evh[j + 1][tk - 1])]
                                          if tk >= 1 else [])
                                if tk == 2 * w and dv_hist:
                                    vchain = vchain + [(ve.h, dv_hist[-1])]
                                qkv_v_tile(j + 1, tk,
                                           (lambda hv=hv, i2=tk - 2 * w: lambda k:
                                            hv[:, k * 256 + i2 * 128:
                                                k * 256 + (i2 + 1) * 128])(),
                                           pvv, vchain)
                        else:
                            qkv_proj(j + 1, w,
                                     (lambda hv=hv: lambda k:
                                      hv[:, k * 256:(k + 1) * 256])(),
                                     "q", pqk, [(ve.h, qk_evac[(j + 1, "k", w)])])
                            hw_pe.append(pe.v)
                            if g + 2 < (NP - 1) * NW:
                                hwin_load(g + 2)
                            if c == 2 * NW:
                                qkv_pe_done[j + 1] = pe.v
                        lwv[0] = ve.v
                        flush_bq()
                    if c >= 1:
                        ost = ostg[ca % 2]
                        P.emit("v", mk_nop(), wait=[ve], wait_raw=[(pe.h, av_hist[-1])])
                        if ag >= 2:
                            P.emit("v", mk_nop(), wait_raw=[dm_hist[ag - 2]])
                        for h01 in range(2):
                            ocol = h01 * VSLOT
                            P.emit("v", mk_recip(rec[:, h01:h01 + 1], po[:, ocol + 64:ocol + 65]))
                        P.emit("v", mk_nop(), wait=[ve])
                        for h01 in range(2):
                            ocol = h01 * VSLOT
                            P.emit("v", mk_ts(ost[:, h01 * DH:(h01 + 1) * DH],
                                              po[:, ocol:ocol + 64],
                                              rec[:, h01:h01 + 1], "mult"))
                        P.emit("v", mk_nop())
                        dv_hist.append(ve.v)
                        bo_ = SQ.batch([(on_d[ca, :, 2 * j * DH:(2 * j + 2) * DH], ost[:, :])],
                                       wait=[ve])
                        dm_hist.append(bo_)
            AR.reset(m2)

        if "p2" in phases:
            emitP2()
        barrier()

        def emitP3():
            CB = 512
            NCB = S // CB
            NT = CB // 128
            ffw_s = AR.bf16(128, ND * D)
            lw_s = AR.bf16(128, 2 * 2 * D)
            m3 = AR.mark()
            wo_s = AR.bf16(128, ND * D)
            cm_s = AR.bf16(128, 2 * ND * G)
            mW = AR.mark()
            wtmps = [AR.f32(128, D) for _ in range(2)]
            jobs = ([(wo, wo_s, k * D, D) for k in range(ND)]
                    + [(ffw, ffw_s, k * D, D) for k in range(ND)]
                    + [(cmso, cm_s, k * G, G) for k in range(ND)]
                    + [(cmsi, cm_s, 2048 + k * G, G) for k in range(ND)]
                    + [(lwso, lw_s, gc * D, D) for gc in range(2)]
                    + [(lwsi, lw_s, 2048 + gc * D, D) for gc in range(2)])
            srcrow = ([(wo, k) for k in range(ND)] + [(ffw, k) for k in range(ND)]
                      + [(cmso, k) for k in range(ND)] + [(cmsi, k) for k in range(ND)]
                      + [(lwso, gc) for gc in range(2)] + [(lwsi, gc) for gc in range(2)])
            cast_hist = []
            for i, ((wt, dst, off, w), (wt2, row)) in enumerate(zip(jobs, srcrow)):
                wr = [cast_hist[i - 2]] if i >= 2 else []
                bw = LQ.batch([(wtmps[i % 2][:, 0:w], wt[row * 128:(row + 1) * 128, :])],
                              wait_raw=wr)
                P.emit("v", mk_copy(dst[:, off:off + w], wtmps[i % 2][:, 0:w]),
                       wait_raw=[bw], inc=[ve])
                cast_hist.append((ve.h, ve.v))
            AR.reset(mW)
            onrm = AR.bf16(128, NT * D)
            oT_c = AR.bf16(128, ND * CB)
            hT_c = AR.f32(128, ND * CB)
            x_c2 = AR.f32(128, ND * CB)
            y2 = [AR.f32(128, ND * CB) for _ in range(2)]
            ybf = AR.bf16(128, ND * CB)
            st2 = AR.f32(1, 3 * CB)
            rst2 = AR.f32(1, CB)
            u2 = [AR.bf16(128, ND * CB) for _ in range(2)]
            fubf = AR.bf16(128, ND * CB)
            xcsq = AR.f32(128, 4 * CB)
            xcst = AR.f32(128, 2)

            ptrs3 = [PE2[:, 0:64].bitcast(BF16), PD[:, 128:192].bitcast(BF16)]
            pp = PD[:, 0:512]
            pst2a = PB[0:1, 0:CB]
            pst2b = PB[0:1, 512:512 + CB]
            pbc2 = PC[:, 0:2 * CB]
            pcv = PA[:, 0:CB]
            pfu = PA[:, 512:512 + CB]

            def ln_stats_bcast(srcf32, wr=()):
                P.emit("s", mk_nop(), wait=[pe], wait_raw=wr)
                for n in range(ND):
                    P.emit("s", mk_act(ybf[:, n * CB:(n + 1) * CB],
                                       srcf32[:, n * CB:(n + 1) * CB], AF.Copy))
                P.emit("s", mk_nop())
                P.emit("t", mk_nop(), wait=[ac])
                for k in range(ND):
                    P.emit("t", mk_mm(pst2a, ones_c[:, 0:1], ybf[:, k * CB:(k + 1) * CB],
                                      k == 0, k == ND - 1, skip=True),
                           inc=([pe] if k == ND - 1 else []))
                P.emit("s", mk_nop(), wait=[pe])
                for n in range(ND):
                    P.emit("s", mk_act(ybf[:, n * CB:(n + 1) * CB], ybf[:, n * CB:(n + 1) * CB],
                                       AF.Square))
                P.emit("s", mk_nop())
                P.emit("t", mk_nop(), wait=[ac])
                for k in range(ND):
                    P.emit("t", mk_mm(pst2b, ones_c[:, 0:1], ybf[:, k * CB:(k + 1) * CB],
                                      k == 0, k == ND - 1, skip=True),
                           inc=([pe] if k == ND - 1 else []))
                P.emit("v", mk_ts(st2[0:1, 0:CB], pst2a, 1.0 / D, "mult"), wait=[pe])
                P.emit("v", mk_ts(st2[0:1, CB:2 * CB], pst2b, 1.0 / D, "mult"))
                P.emit("v", mk_tt(rst2[0:1, 0:CB], st2[0:1, 0:CB], st2[0:1, 0:CB], "mult"),
                       wait=[ve])
                P.emit("v", mk_tt(st2[0:1, 2 * CB:3 * CB], st2[0:1, CB:2 * CB],
                                  rst2[0:1, 0:CB], "subtract"), wait=[ve])
                P.emit("v", mk_ts(st2[0:1, 2 * CB:3 * CB], st2[0:1, 2 * CB:3 * CB], EPS, "add"),
                       wait=[ve])
                P.emit("s", mk_act(rst2[0:1, 0:CB], st2[0:1, 2 * CB:3 * CB], AF.Ln), wait=[ve])
                P.emit("s", mk_act(st2[0:1, 2 * CB:3 * CB], rst2[0:1, 0:CB], AF.Exp, scale=-0.5),
                       wait=[ac])
                P.emit("v", mk_tt(st2[0:1, CB:2 * CB], st2[0:1, 0:CB],
                                  st2[0:1, 2 * CB:3 * CB], "mult"), wait=[ac])
                P.emit("t", mk_mm(pbc2[:, 0:CB], ones_r[0:1, :], st2[0:1, 2 * CB:3 * CB],
                                  True, True), wait=[ve])
                P.emit("t", mk_mm(pbc2[:, CB:2 * CB], ones_r[0:1, :], st2[0:1, CB:2 * CB],
                                  True, True), inc=[pe])

            def ld_p3(c, w=(), wr=()):
                t0, t1 = c * CB, (c + 1) * CB
                prs = ([(onrm[:, i * D:(i + 1) * D], on_d[c * NT + i, :, :]) for i in range(NT)]
                       + [(hT_c[:, k * CB:(k + 1) * CB], hT_d[:, k, t0:t1]) for k in range(ND)]
                       + [(x_c2[:, k * CB:(k + 1) * CB], xT[:, k, t0:t1]) for k in range(ND)])
                return tuple(LQ.batch(prs[i::4], wait=w, wait_raw=wr) for i in range(4))

            bc3 = ld_p3(0, w=[pe, ve])
            sttv = {}
            bx1h = {}
            bxh = {}
            for c in range(NCB + 2):
              if c < NCB:
                t0, t1 = c * CB, (c + 1) * CB
                P.emit("t", mk_nop(), wait=[ve], wait_raw=list(bc3))
                trh = []
                for i in range(NT):
                    for k in range(ND):
                        idx = i * ND + k
                        if idx >= 2:
                            P.emit("t", mk_nop(), wait_raw=[(ve.h, trh[idx - 2])])
                        P.emit("t", mk_trans(ptrs3[idx % 2],
                                             onrm[:, i * D + k * 128:i * D + (k + 1) * 128],
                                             ident_b[:, :]), inc=[pe])
                        P.emit("v", mk_copy(oT_c[:, k * CB + i * 128:k * CB + (i + 1) * 128],
                                            ptrs3[idx % 2]), wait=[pe])
                        trh.append(ve.v)
                b_on = None
                if c + 1 < NCB:
                    prs_on = [(onrm[:, i * D:(i + 1) * D], on_d[(c + 1) * NT + i, :, :])
                              for i in range(NT)]
                    b_on = [LQ.batch(prs_on[i::2], wait_raw=[(pe.h, pe.v)])
                            for i in range(2)]
                P.emit("t", mk_nop(), wait=[ve])
                yb = y2[c % 2]
                for n in range(ND):
                    for k in range(ND):
                        P.emit("t", mk_mm(pp, wo_s[:, k * D + n * 128:k * D + (n + 1) * 128],
                                          oT_c[:, k * CB:(k + 1) * CB], k == 0, k == ND - 1),
                               inc=([pe] if k == ND - 1 else []))
                    P.emit("v", mk_stt(yb[:, n * CB:(n + 1) * CB], pp, bo_s[:, n:n + 1],
                                       hT_c[:, n * CB:(n + 1) * CB], "add", "add"), wait=[pe])
                    P.emit("t", mk_nop(), wait=[ve])
                sttv[c] = ve.v
                b_h3 = None
                if c + 1 < NCB:
                    prs_h3 = [(hT_c[:, k * CB:(k + 1) * CB],
                               hT_d[:, k, (c + 1) * CB:(c + 2) * CB])
                              for k in range(ND)]
                    b_h3 = [LQ.batch(prs_h3[i::2], wait=[ve]) for i in range(2)]
                if c + 1 < NCB:
                    bc3 = tuple(b_on) + tuple(b_h3)
              if 1 <= c <= NCB:
                cl = c - 1
                tl0, tl1 = cl * CB, (cl + 1) * CB
                yb = y2[cl % 2]
                ln_stats_bcast(yb, wr=[(ve.h, sttv[cl])])
                P.emit("v", mk_nop(), wait=[pe], wait_raw=[(ve.h, sttv[cl])]
                       + ([bxh[cl]] if cl in bxh else []))
                for n in range(ND):
                    P.emit("v", mk_tt(yb[:, n * CB:(n + 1) * CB], yb[:, n * CB:(n + 1) * CB],
                                      pbc2[:, 0:CB], "mult"))
                P.emit("v", mk_nop(), wait=[ve])
                for n in range(ND):
                    P.emit("v", mk_tt(yb[:, n * CB:(n + 1) * CB], yb[:, n * CB:(n + 1) * CB],
                                      pbc2[:, CB:2 * CB], "subtract"))
                P.emit("v", mk_nop(), wait=[ve])
                for n in range(ND):
                    P.emit("v", mk_ts2(yb[:, n * CB:(n + 1) * CB], yb[:, n * CB:(n + 1) * CB],
                                       g2_s[:, n:n + 1], "mult", b2_s[:, n:n + 1], "add"))
                P.emit("v", mk_nop(), wait=[ve])
                for n in range(ND):
                    P.emit("v", mk_tt(yb[:, n * CB:(n + 1) * CB], yb[:, n * CB:(n + 1) * CB],
                                      x_c2[:, n * CB:(n + 1) * CB], "add"))
                P.emit("v", mk_nop())
                bx1 = SQ.batch([(x1_d[:, k, tl0:tl1], yb[:, k * CB:(k + 1) * CB])
                                for k in range(ND)], wait=[ve])
                bx1h[cl] = bx1
                if cl + 1 < NCB:
                    bxh[cl + 1] = LQ.batch([(x_c2[:, k * CB:(k + 1) * CB],
                                             xT[:, k, (cl + 1) * CB:(cl + 2) * CB])
                                            for k in range(ND)], wait=[ve])
                ln_stats_bcast(yb, wr=[(ve.h, ve.v)])
                ub = u2[cl % 2]
                P.emit("v", mk_nop(), wait=[pe], wait_raw=[bx1])
                for n in range(ND):
                    P.emit("v", mk_tt(yb[:, n * CB:(n + 1) * CB], yb[:, n * CB:(n + 1) * CB],
                                      pbc2[:, 0:CB], "mult"))
                P.emit("v", mk_nop(), wait=[ve])
                for n in range(ND):
                    P.emit("v", mk_tt(ub[:, n * CB:(n + 1) * CB], yb[:, n * CB:(n + 1) * CB],
                                      pbc2[:, CB:2 * CB], "subtract"))
                P.emit("v", mk_nop())
                SQ.batch([(uT_d[:, k, tl0:tl1], ub[:, k * CB:(k + 1) * CB])
                          for k in range(ND)], wait=[ve])
              if 2 <= c:
                cf = c - 2
                tf0, tf1 = cf * CB, (cf + 1) * CB
                uf = u2[cf % 2]
                bxc = {}
                P.emit("t", mk_nop(), wait=[ve])
                for br in range(2):
                    for gc in range(2):
                        for k in range(ND):
                            P.emit("t", mk_mm(pcv, cm_s[:, br * 2048 + k * G + gc * 128:
                                                        br * 2048 + k * G + (gc + 1) * 128],
                                              uf[:, k * CB:(k + 1) * CB], k == 0, k == ND - 1),
                                   inc=([pe] if k == ND - 1 else []))
                        col = 2 * br + gc
                        P.emit("v", mk_ts(xcsq[:, col * CB:(col + 1) * CB], pcv,
                                          cb_s[:, col:col + 1], "add"), wait=[pe])
                        P.emit("t", mk_nop(), wait=[ve])
                        bxc[col] = SQ.batch([(xc_d[:, col, tf0:tf1],
                                              xcsq[:, col * CB:(col + 1) * CB])], wait=[ve])
                P.emit("v", mk_nop(), wait=[ve])
                for col in range(4):
                    P.emit("v", mk_red(xcst[:, 0:1], xcsq[:, col * CB:(col + 1) * CB], "add"))
                    P.emit("v", mk_tt(bnacc[:, col:col + 1], bnacc[:, col:col + 1],
                                      xcst[:, 0:1], "add"), wait=[ve])
                    P.emit("v", mk_tt(xcsq[:, col * CB:(col + 1) * CB],
                                      xcsq[:, col * CB:(col + 1) * CB],
                                      xcsq[:, col * CB:(col + 1) * CB], "mult"),
                           wait=[ve], wait_raw=[bxc[col]])
                    P.emit("v", mk_red(xcst[:, 1:2], xcsq[:, col * CB:(col + 1) * CB], "add"),
                           wait=[ve])
                    P.emit("v", mk_tt(bnacc[:, 4 + col:5 + col], bnacc[:, 4 + col:5 + col],
                                      xcst[:, 1:2], "add"), wait=[ve])
                P.emit("t", mk_nop(), wait=[ve])
                for n in range(ND):
                    for k in range(ND):
                        P.emit("t", mk_mm(pfu, ffw_s[:, k * D + n * 128:k * D + (n + 1) * 128],
                                          uf[:, k * CB:(k + 1) * CB], k == 0, k == ND - 1),
                               inc=([pe] if k == ND - 1 else []))
                    P.emit("v", mk_copy(fubf[:, n * CB:(n + 1) * CB], pfu), wait=[pe])
                    P.emit("t", mk_nop(), wait=[ve])
                SQ.batch([(fu_d[:, k, tf0:tf1], fubf[:, k * CB:(k + 1) * CB])
                          for k in range(ND)], wait=[ve])
            AR.reset(m3)
            return ffw_s, lw_s

        AR.reset(M_PHASE)
        if "p3" in phases:
            ffw_s, lw_s = emitP3()
        else:
            ffw_s = AR.bf16(128, ND * D); lw_s = AR.bf16(128, 2 * 2 * D)
        barrier()

        def emitCC():
            b1 = SQ.batch([(cc_i[:, :], bnacc[:, :])])
            P.emit("g", mk_nop(), wait_raw=[b1])
            P.emit("g", mk_coll(cc_i[:, :], cc_o[:, :], ccs.h), noauto=True)
            ccs.v = 1
            P.emit("g", mk_nop(), wait_raw=[(ccs.h, 1)])
            b2 = SQ.batch([(bnacc[:, :], cc_o[:, :])])
            P.emit("v", mk_nop(), wait_raw=[b2])
            tmp = AR.f32(128, 4)
            for col in range(4):
                P.emit("v", mk_ts(tmp[:, 0:1], bnacc[:, col:col + 1], 1.0 / 32768.0, "mult"))
                P.emit("v", mk_ts(tmp[:, 1:2], bnacc[:, 4 + col:5 + col], 1.0 / 32768.0, "mult"),
                       wait=[ve])
                P.emit("v", mk_tt(tmp[:, 2:3], tmp[:, 0:1], tmp[:, 0:1], "mult"),
                       wait=[ve])
                P.emit("v", mk_tt(tmp[:, 1:2], tmp[:, 1:2], tmp[:, 2:3], "subtract"),
                       wait=[ve])
                P.emit("v", mk_ts(tmp[:, 1:2], tmp[:, 1:2], EPS, "add"), wait=[ve])
                P.emit("s", mk_act(tmp[:, 2:3], tmp[:, 1:2], AF.Ln), wait=[ve])
                P.emit("s", mk_act(tmp[:, 3:4], tmp[:, 2:3], AF.Exp, scale=-0.5),
                       wait=[ac])
                P.emit("v", mk_tt(bnp_sc[:, col:col + 1], tmp[:, 3:4], gg_s[:, col:col + 1],
                                  "mult"), wait=[ac])
                P.emit("v", mk_tt(tmp[:, 0:1], tmp[:, 0:1], bnp_sc[:, col:col + 1], "mult"),
                       wait=[ve])
                P.emit("v", mk_tt(bnp_bi[:, col:col + 1], bb_s[:, col:col + 1], tmp[:, 0:1],
                                  "subtract"), wait=[ve])

        if "cc" in phases and "fb" not in phases:
            emitCC()
            barrier()

        def emitFB():
            CB = 512
            NCB = S // CB
            NT = CB // 128
            NTL = 2 * NT
            m4 = AR.mark()
            xc_c = AR.f32(128, 4 * CB)
            fu_c = AR.bf16(128, ND * CB)
            u_c = AR.bf16(128, ND * CB)
            x1_c = AR.f32(128, ND * CB)
            xa = AR.bf16(128, 4 * CB)
            e_s = [AR.f32(128, D) for _ in range(NTL)]
            z0 = [AR.f32(128, D) for _ in range(NTL)]
            m8 = AR.f32(128, 8 * NTL)
            den = AR.f32(128, 2 * NTL)
            gbf = [AR.bf16(128, D) for _ in range(2)]
            gT = [AR.bf16(128, ND * CB) for _ in range(2)]
            ug = AR.bf16(128, ND * CB)
            tmb = AR.f32(128, ND * CB)
            yo = tmb

            pLs = [PA[:, 0:1024], PC[:, 0:1024]]
            ptrs = [PE2[:, 0:64].bitcast(BF16), PD[:, 0:64].bitcast(BF16)]
            pI = [PB[:, 0:CB], PB[:, 512:512 + CB]]

            def ld_fb(tn, w=(), wr=()):
                prs = ([(xc_c.rearrange("p (q t) -> p q t", q=4), xc_d[:, :, tn:tn + CB])]
                       + [(fu_c[:, k * CB:(k + 1) * CB], fu_d[:, k, tn:tn + CB])
                          for k in range(ND)]
                       + [(u_c[:, k * CB:(k + 1) * CB], uT_d[:, k, tn:tn + CB])
                          for k in range(ND)])
                return [LQ.batch(prs[i::4], wait=w, wait_raw=wr) for i in range(4)]

            bfb = ld_fb(0, w=[ve])
            bx1b = [LQ.batch([(x1_c[:, k * CB:(k + 1) * CB], x1_d[:, k, 0:CB])
                              for k in range(ND)][i::2], wait=[ve]) for i in range(2)]
            if "cc" in phases:
                emitCC()
            barrier()
            exp_done = []
            sub_done = {}
            xa_free = {}
            gelu_ac = {}
            for c in range(NCB):
                t0, t1 = c * CB, (c + 1) * CB
                P.emit("s", mk_nop(),
                       wait_raw=list(bfb) + ([(pe.h, xa_free[c - 1])] if c else []))
                for col in range(4):
                    P.emit("s", mk_act(xa[:, col * CB:(col + 1) * CB],
                                       xc_c[:, col * CB:(col + 1) * CB], AF.Gelu,
                                       scale=bnp_sc[:, col:col + 1],
                                       bias=bnp_bi[:, col:col + 1]))
                P.emit("s", mk_nop())
                gelu_ac[c] = ac.v
                P.emit("t", mk_nop(), wait_raw=[(ac.h, gelu_ac[c])])
                for ti in range(NTL):
                    br, tt = ti // NT, ti % NT
                    pL = pLs[ti % 2]
                    gti = c * NTL + ti
                    if gti >= 2:
                        P.emit("t", mk_nop(), wait_raw=[(ac.h, exp_done[gti - 2])])
                    for half in range(2):
                        for gc in range(2):
                            col = 2 * br + gc
                            P.emit("t", mk_mm(pL[:, half * 512:(half + 1) * 512],
                                              xa[:, col * CB + tt * 128:col * CB + (tt + 1) * 128],
                                              lw_s[:, br * 2048 + gc * D + half * 512:
                                                   br * 2048 + gc * D + (half + 1) * 512],
                                              gc == 0, False))
                        P.emit("t", mk_mm(pL[:, half * 512:(half + 1) * 512],
                                          ones_rb[0:1, :],
                                          lb_b[0:1, br * D + half * 512:br * D + (half + 1) * 512],
                                          False, True), inc=([pe] if half == 1 else []))
                    if ti == NTL - 1:
                        xa_free[c] = pe.v
                    P.emit("s", mk_act(e_s[ti][:, :], pL, AF.Exp), wait=[pe],
                           wait_raw=([(ve.h, sub_done[(c - 1, ti)])] if c else []))
                    exp_done.append(ac.v)
                P.emit("v", mk_nop(), wait=[ac])
                for ti in range(NTL):
                    P.emit("v", mk_red(den[:, 2 * ti:2 * ti + 1], e_s[ti][:, :], "add"))
                P.emit("v", mk_nop(), wait=[ve])
                for ti in range(NTL):
                    P.emit("v", mk_recip(den[:, 2 * ti + 1:2 * ti + 2],
                                         den[:, 2 * ti:2 * ti + 1]))
                for r in range(4):
                    P.emit("v", mk_nop(), wait=[ve])
                    for ti in range(NTL):
                        srcz = e_s[ti] if r == 0 else z0[ti]
                        P.emit("v", mk_max8(m8[:, 8 * ti:8 * (ti + 1)], srcz[:, :]))
                    P.emit("v", mk_nop(), wait=[ve])
                    for ti in range(NTL):
                        srcz = e_s[ti] if r == 0 else z0[ti]
                        P.emit("v", mk_mr(z0[ti][:, :], m8[:, 8 * ti:8 * (ti + 1)],
                                          srcz[:, :], 0.0))
                P.emit("v", mk_nop(), wait=[ve])
                tr_hist = []
                for ti in range(NTL):
                    br, tt = ti // NT, ti % NT
                    P.emit("v", mk_tt(e_s[ti][:, :], e_s[ti][:, :], z0[ti][:, :], "subtract"),
                           wait=([ve] if ti == 0 else ()))
                    gb = gbf[ti % 2]
                    if ti >= 2:
                        P.emit("v", mk_nop(), wait_raw=[(pe.h, tr_hist[(ti - 2) * ND + ND - 1][0])])
                    P.emit("v", mk_ts(gb[:, :], e_s[ti][:, :],
                                      den[:, 2 * ti + 1:2 * ti + 2], "mult"), wait=[ve])
                    sub_done[(c, ti)] = ve.v
                    for k in range(ND):
                        i = ti * ND + k
                        if i >= 2:
                            P.emit("t", mk_nop(), wait_raw=[(ac.h, tr_hist[i - 2][1])])
                        P.emit("t", mk_trans(ptrs[i % 2], gb[:, k * 128:(k + 1) * 128],
                                             ident_b[:, :]),
                               wait=([ve] if k == 0 else ()), inc=[pe])
                        P.emit("s", mk_act(gT[br][:, k * CB + tt * 128:k * CB + (tt + 1) * 128],
                                           ptrs[i % 2], AF.Copy), wait=[pe])
                        tr_hist.append((pe.v, ac.v))
                P.emit("g", mk_nop(), wait=[ve, pe, ac])
                for k in range(ND):
                    P.emit("g", mk_tt(ug[:, k * CB:(k + 1) * CB], u_c[:, k * CB:(k + 1) * CB],
                                      gT[1][:, k * CB:(k + 1) * CB], "mult"))
                for n in range(ND):
                    P.emit("g", mk_tt(tmb[:, n * CB:(n + 1) * CB], fu_c[:, n * CB:(n + 1) * CB],
                                      gT[0][:, n * CB:(n + 1) * CB], "mult"))
                P.emit("g", mk_nop())
                if c + 1 < NCB:
                    bfb = ld_fb((c + 1) * CB, w=[ve, gp])
                P.emit("t", mk_nop(), wait=[ve, gp])
                for n in range(ND):
                    pi = pI[n % 2]
                    for k in range(ND):
                        P.emit("t", mk_mm(pi, ffw_s[:, k * D + n * 128:k * D + (n + 1) * 128],
                                          ug[:, k * CB:(k + 1) * CB], k == 0, k == ND - 1),
                               inc=([pe] if k == ND - 1 else []))
                    P.emit("v", mk_tt(tmb[:, n * CB:(n + 1) * CB], tmb[:, n * CB:(n + 1) * CB],
                                      pi, "add"), wait=[pe, ve, gp])
                    P.emit("t", mk_nop(), wait=[ve])
                    P.emit("v", mk_stt(yo[:, n * CB:(n + 1) * CB], tmb[:, n * CB:(n + 1) * CB],
                                       ffb_s[:, n:n + 1], x1_c[:, n * CB:(n + 1) * CB],
                                       "add", "add"), wait=[ve],
                           wait_raw=(list(bx1b) if n == 0 else ()))
                P.emit("g", mk_nop(), wait=[ve])
                SQ.batch([(yout[:, k, t0:t1], yo[:, k * CB:(k + 1) * CB])
                          for k in range(ND)], wait=[ve])
                if c + 1 < NCB:
                    tn = (c + 1) * CB
                    bx1b = [LQ.batch([(x1_c[:, k * CB:(k + 1) * CB], x1_d[:, k, tn:tn + CB])
                                      for k in range(ND)][i::2], wait=[ve])
                            for i in range(2)]
                P.emit("v", mk_nop(), wait=SQA)
            AR.reset(m4)

        if "fb" in phases:
            emitFB()
        barrier()
        P.emit("g", mk_nop(), wait=SQA + LQA)

        with nc.Block() as block:
            block.tensor(lambda T: [f(T) for f in P.ops["t"]])
            block.vector(lambda V: [f(V) for f in P.ops["v"]])
            block.scalar(lambda A: [f(A) for f in P.ops["s"]])
            block.gpsimd(lambda GP: [f(GP) for f in P.ops["g"]])
            block.sync(lambda SY: [f(SY) for f in P.ops["y"]])
    return nc



def _pe_table():
    pos = np.arange(S, dtype=np.float64)[:, None]
    div = np.exp(np.arange(0, D, 2, dtype=np.float64) * (-math.log(10000.0) / D))
    pe_t = np.zeros((S, D), dtype=np.float32)
    pe_t[:, 0::2] = np.sin(pos * div).astype(np.float32)
    pe_t[:, 1::2] = np.cos(pos * div).astype(np.float32)
    return pe_t


def _feat_major(a2d):
    return np.ascontiguousarray(a2d.T.reshape(ND, 128, S).transpose(1, 0, 2))


def _col_major_bias(b):
    return np.ascontiguousarray(b.reshape(ND, 128).T)


def _embed_conv(cw):
    cm = np.zeros((D, G), dtype=np.float32)
    g = np.arange(G)
    for f in range(4):
        cm[g * 4 + f, g] = cw[:, f]
    return cm


def kernel(**inp):
    x = np.asarray(inp["L0_attn_x"], dtype=np.float32)
    pe_t = _pe_table()

    def f32(v):
        return np.ascontiguousarray(np.asarray(v, dtype=np.float32))

    ctab = np.zeros((128, 16), np.float32)
    for col, val in {0: 1.0, 1: 0.125, 2: EXP_BIAS, 3: 1.0 / D, 4: -0.5, 5: EPS,
                     6: 1.0 / 32768.0, 7: 0.0, 8: -1.0}.items():
        ctab[:, col] = val

    shared = {
        "pet": _feat_major(pe_t),
        "wq": f32(inp["L0_attn_Wq"]), "wk": f32(inp["L0_attn_Wk"]),
        "wv": f32(inp["L0_attn_Wv"]), "wo": f32(inp["L0_attn_Wo"]),
        "bq": _col_major_bias(f32(inp["L0_attn_bq"])),
        "bk": _col_major_bias(f32(inp["L0_attn_bk"])),
        "bvr": f32(inp["L0_attn_bv"])[None, :],
        "bo": _col_major_bias(f32(inp["L0_attn_bo"])),
        "gam2": _col_major_bias(f32(inp["L0_attn_gamma"])),
        "bet2": _col_major_bias(f32(inp["L0_attn_beta"])),
        "ffw": f32(inp["L0_ffn_weight"]),
        "ffb": _col_major_bias(f32(inp["L0_ffn_bias"])),
        "cmso": _embed_conv(f32(inp["L0_ffn_so_conv_w"])),
        "cmsi": _embed_conv(f32(inp["L0_ffn_si_conv_w"])),
        "cbso": np.ascontiguousarray(f32(inp["L0_ffn_so_conv_b"]).reshape(2, 128).T),
        "cbsi": np.ascontiguousarray(f32(inp["L0_ffn_si_conv_b"]).reshape(2, 128).T),
        "gso": np.ascontiguousarray(f32(inp["L0_ffn_so_gamma"]).reshape(2, 128).T),
        "bso": np.ascontiguousarray(f32(inp["L0_ffn_so_beta"]).reshape(2, 128).T),
        "gsi": np.ascontiguousarray(f32(inp["L0_ffn_si_gamma"]).reshape(2, 128).T),
        "bsi": np.ascontiguousarray(f32(inp["L0_ffn_si_beta"]).reshape(2, 128).T),
        "lwso": np.ascontiguousarray(f32(inp["L0_ffn_so_lin_w"]).T),
        "lwsi": np.ascontiguousarray(f32(inp["L0_ffn_si_lin_w"]).T),
        "lbso": f32(inp["L0_ffn_so_lin_b"])[None, :],
        "lbsi": f32(inp["L0_ffn_si_lin_b"])[None, :],
        "identf": np.eye(128, dtype=np.float32),
        "ctab": ctab,
        "onesrow": np.ones((1, 128), np.float32),
    }

    nc = build_nc()
    in_maps = []
    for i in range(NCORES):
        m = dict(shared)
        m["xT"] = _feat_major(x[i])
        in_maps.append(m)
    res = run_bass_kernel_spmd(nc, in_maps, core_ids=list(range(NCORES)))
    outs = []
    for i in range(NCORES):
        y = np.asarray(res.results[i]["yout"])
        outs.append(y.transpose(1, 0, 2).reshape(D, S).T)
    return np.stack(outs, axis=0)


if __name__ == "__main__":
    rng = np.random.default_rng(0)
    print("building program ...")
    nc = build_nc()
    print("built")


# version 25
# speedup vs baseline: 1.0079x; 1.0079x over previous
import math
from contextlib import ExitStack

import numpy as np
import concourse.bass as bass
import concourse.mybir as mybir
from concourse.bass_utils import run_bass_kernel_spmd

F32 = mybir.dt.float32
BF16 = mybir.dt.bfloat16
AF = mybir.ActivationFunctionType
ALU = mybir.AluOpType
AX = mybir.AxisListType

S = 4096
D = 1024
H = 16
DH = 64
G = 256
ND = 8
NP = 8
CH = 128
NCH = S // CH
NTK = S // 128
VSLOT = 66
EXP_BIAS = -10.0
EPS = 1e-5
NCORES = 8

ARENA_COLS = 49152


class Sem:
    def __init__(self, ctx, nc, name):
        self.h = ctx.enter_context(nc.semaphore(name))
        self.v = 0


class Arena:

    def __init__(self, h):
        self.h = h
        self.top = 0

    def f32(self, rows, cols):
        o = self.top
        self.top += cols
        assert self.top <= ARENA_COLS, f"SBUF arena overflow: {self.top}"
        return self.h[0:rows, o:o + cols]

    def bf16(self, rows, cols):
        o = self.top
        self.top += (cols + 1) // 2
        assert self.top <= ARENA_COLS, f"SBUF arena overflow: {self.top}"
        return self.h[0:128, o:o + (cols + 1) // 2].bitcast(BF16)[0:rows, 0:cols]

    def mark(self):
        return self.top

    def reset(self, m):
        self.top = m


class Prog:

    KEYS = ("t", "v", "s", "g", "y")
    DMA_DEPTH = 8

    def __init__(self):
        self.ops = {k: [] for k in self.KEYS}
        self.dmacnt = {}

    def set_dma_sems(self, sems):
        self.dma_hs = {id(s.h) for s in sems}
        self.dma_h2 = {s.h for s in sems}

    def __init_sems__(self, esem):
        self.esem = esem
        self.bneed = set()

    def emit(self, key, fn, wait=(), inc=(), inc16=(), wait_raw=(), noauto=False):
        waits = tuple((s.h, s.v) for s in wait) + tuple(wait_raw)
        for s in wait:
            if s in self.esem.values():
                self.bneed.add(s)
        own = self.esem.get(key)
        auto = key in ("v", "s", "g") and not noauto
        if own is not None and own in self.bneed:
            if not waits:
                waits = ((own.h, own.v),)
            self.bneed.discard(own)
        step = 16 if key in ("g", "y") else 1
        pre = tuple((s.h, s.v) for s in inc)
        incs = tuple([(s.h, step) for s in inc] + [(s.h, 16) for s in inc16])
        if auto and not incs:
            incs = ((own.h, step),)
            own.v += step
        if incs and not inc16:
            dma_hs = getattr(self, "dma_hs", set())
            if not any(id(h) not in dma_hs for h, v in waits):
                sw = (own.h, own.v - (step if (auto or own in inc) else 0)) \
                    if own is not None else (pre[0] if pre else (incs[0][0], 0))
                waits = waits + (sw,)
        for s in inc:
            s.v += step
        for s in inc16:
            s.v += 16

        def cl(eng, fn=fn, waits=waits, incs=incs):
            for h, v in waits:
                eng.wait_ge(h, v)
            inst = fn(eng)
            for h, n in incs:
                inst.then_inc(h, n)

        self.ops[key].append(cl)


class DmaQ:

    def __init__(self, P, key, sems):
        self.P, self.key, self.sems, self.i = P, key, sems, 0

    def batch(self, pairs, wait=(), wait_raw=()):
        s = self.sems[self.i % len(self.sems)]
        self.i += 1
        pairs = list(pairs)
        for n, (dst, srcap) in enumerate(pairs):
            self.P.emit(self.key, mk_dma(dst, srcap),
                        wait=(tuple(wait) + (s,)) if n == 0 else (),
                        wait_raw=wait_raw if n == 0 else (), inc16=[s])
        return (s.h, s.v)



def mk_nop():
    return lambda e: e.nop()


def mk_dma(dst, src):
    return lambda e: e.dma_start(out=dst, in_=src)


def mk_mm(o, a, b, st, sp, tp=None, skip=False):
    if tp is None:
        return lambda T: T.matmul(o, a, b, start=st, stop=sp, skip_group_check=skip)
    return lambda T: T.matmul(o, a, b, start=st, stop=sp, tile_position=tp,
                              skip_group_check=skip)


def mk_trans(o, i, ident):
    return lambda T: T.transpose(o, i, ident)


def mk_act(o, i, func, scale=None, bias=None):
    kw = {}
    if scale is not None:
        kw["scale"] = scale
    if bias is not None:
        kw["bias"] = bias
    return lambda A: A.activation(out=o, in_=i, func=func, **kw)


def mk_copy(o, i):
    return lambda V: V.tensor_copy(out=o, in_=i)


def mk_tt(o, a, b, opname):
    op = getattr(ALU, opname)
    return lambda V: V.tensor_tensor(out=o, in0=a, in1=b, op=op)


def mk_ts(o, a, s1, opname):
    op = getattr(ALU, opname)
    return lambda V: V.tensor_scalar(out=o, in0=a, scalar1=s1, scalar2=None, op0=op)


def mk_ts2(o, a, s1, op0name, s2, op1name):
    op0 = getattr(ALU, op0name)
    op1 = getattr(ALU, op1name)
    return lambda V: V.tensor_scalar(out=o, in0=a, scalar1=s1, scalar2=s2,
                                     op0=op0, op1=op1)


def mk_stt(o, a, s1, b, op0name, op1name):
    op0 = getattr(ALU, op0name)
    op1 = getattr(ALU, op1name)
    return lambda V: V.scalar_tensor_tensor(out=o, in0=a, scalar=s1, in1=b,
                                            op0=op0, op1=op1)


def mk_red(o, i, opname):
    op = getattr(ALU, opname)
    return lambda V: V.tensor_reduce(out=o, in_=i, axis=AX.X, op=op)


def mk_recip(o, i):
    return lambda V: V.reciprocal(out=o, in_=i)


def mk_max8(o, i):
    return lambda V: V.max(out=o, in_=i)


def mk_mr(o, m8, vals, imm):
    return lambda V: V.match_replace(out=o, in_to_replace=m8, in_values=vals,
                                     imm_value=imm)


def mk_memset(ap, val):
    return lambda V: V.memset(ap, val)


def mk_coll(ins, outs, sem):
    def f(GP):
        i = GP.collective_compute(
            "AllReduce", ALU.add, replica_groups=[list(range(NCORES))],
            ins=[ins], outs=[outs])
        i.then_inc(sem)
        return _NoInc(i)
    return f


class _NoInc:
    def __init__(self, i):
        self.i = i

    def then_inc(self, h, n):
        raise AssertionError("collective inc is managed internally")



def build_nc(debug_out=(), phases=("p01", "p2", "p3", "cc", "fb")):
    nc = bass.Bass("TRN2", target_bir_lowering=False, debug=False)

    def din(name, shape, dt=F32):
        return nc.dram_tensor(name, list(shape), dt, kind="ExternalInput").ap()

    xT = din("xT", (128, ND, S))
    pet = din("pet", (128, ND, S))
    wq = din("wq", (D, D))
    wk = din("wk", (D, D))
    wv = din("wv", (D, D))
    wo = din("wo", (D, D))
    bq = din("bq", (128, ND))
    bk = din("bk", (128, ND))
    bvr = din("bvr", (1, D))
    bo = din("bo", (128, ND))
    gam2 = din("gam2", (128, ND))
    bet2 = din("bet2", (128, ND))
    ffw = din("ffw", (D, D))
    ffb = din("ffb", (128, ND))
    cmso = din("cmso", (D, G))
    cmsi = din("cmsi", (D, G))
    cbso = din("cbso", (128, 2))
    cbsi = din("cbsi", (128, 2))
    gso = din("gso", (128, 2))
    bso = din("bso", (128, 2))
    gsi = din("gsi", (128, 2))
    bsi = din("bsi", (128, 2))
    lwso = din("lwso", (G, D))
    lwsi = din("lwsi", (G, D))
    lbso = din("lbso", (1, D))
    lbsi = din("lbsi", (1, D))
    identf = din("identf", (128, 128))
    ctab = din("ctab", (128, 16))
    onesrow = din("onesrow", (1, 128))

    yout = nc.dram_tensor("yout", [128, ND, S], F32, kind="ExternalOutput").ap()

    def scratch(name, shape, dt):
        kind = "ExternalOutput" if name in debug_out else "Internal"
        return nc.dram_tensor(name, list(shape), dt, kind=kind).ap()

    hT_d = scratch("hT_d", (128, ND, S), F32)
    qT_d = scratch("qT_d", (128, ND, S), BF16)
    hbf_d = scratch("hbf_d", (128, ND, S), BF16)
    on_d = scratch("on_d", (NCH, 128, D), BF16)
    x1_d = scratch("x1_d", (128, ND, S), F32)
    uT_d = scratch("uT_d", (128, ND, S), BF16)
    xc_d = scratch("xc_d", (128, 4, S), F32)
    fu_d = scratch("fu_d", (128, ND, S), BF16)
    cc_i = nc.dram_tensor("cc_i", [128, 8], F32, kind="Internal").ap()
    cc_o = nc.dram_tensor("cc_o", [128, 8], F32, kind="Internal").ap()

    P = Prog()
    est = ExitStack()
    with est:
        arena_t = est.enter_context(nc.sbuf_tensor("arena", [128, ARENA_COLS], F32))
        AR = Arena(arena_t)

        PA = est.enter_context(nc.psum_tensor("PA", [128, 1024], F32))
        PB = est.enter_context(nc.psum_tensor("PB", [128, 1024], F32))
        PC = est.enter_context(nc.psum_tensor("PC", [128, 1024], F32))
        PD = est.enter_context(nc.psum_tensor("PD", [128, 512], F32))
        PE2 = est.enter_context(nc.psum_tensor("PE2", [128, 512], F32))

        lqs = [Sem(est, nc, f"lq{i}") for i in range(4)]
        sqs = [Sem(est, nc, f"sq{i}") for i in range(4)]
        hqs = [Sem(est, nc, f"hq{i}") for i in range(2)]
        LQ = DmaQ(P, "y", lqs)
        SQ = DmaQ(P, "g", sqs)
        HQ = DmaQ(P, "y", hqs)
        HS = DmaQ(P, "s", hqs[0:1])
        LQA = tuple(lqs)
        SQA = tuple(sqs)
        HQA = tuple(hqs)
        pe = Sem(est, nc, "pe")
        ve = Sem(est, nc, "ve")
        ac = Sem(est, nc, "ac")
        gp = Sem(est, nc, "gp")
        ccs = Sem(est, nc, "ccs")
        P.__init_sems__({"t": pe, "v": ve, "s": ac, "g": gp})
        P.set_dma_sems(lqs + sqs + hqs)

        def barrier():
            P.emit("t", mk_nop(), wait=[pe], inc=[pe])
            P.emit("v", mk_nop())
            P.emit("s", mk_nop())
            P.emit("g", mk_nop())
            allsems = [pe, ve, ac, gp] + list(LQA) + list(SQA) + list(HQA)
            for k in P.KEYS:
                P.emit(k, mk_nop(), wait=allsems)

        cons = AR.f32(128, 16)
        ones_c = AR.bf16(128, 2)
        ones_r = AR.f32(1, 128)
        ones_rb = AR.bf16(1, 128)
        ident_b = AR.bf16(128, 128)
        bq_s = AR.f32(128, ND)
        bk_s = AR.f32(128, ND)
        bo_s = AR.f32(128, ND)
        g2_s = AR.f32(128, ND)
        b2_s = AR.f32(128, ND)
        ffb_s = AR.f32(128, ND)
        cb_s = AR.f32(128, 4)
        gg_s = AR.f32(128, 4)
        bb_s = AR.f32(128, 4)
        lb_b = AR.bf16(1, 2 * D)
        vbb = AR.f32(128, D)
        bnacc = AR.f32(128, 8)
        bnp_sc = AR.f32(128, 4)
        bnp_bi = AR.f32(128, 4)

        CONSTS = {0: 1.0, 1: 0.125, 2: EXP_BIAS, 3: 1.0 / D, 4: -0.5, 5: EPS,
                  6: 1.0 / 32768.0, 7: 0.0, 8: -1.0}
        for col, val in CONSTS.items():
            nc.const_aps.register(F32, val, cons[:, col:col + 1])

        M_PHASE = AR.mark()
        v_sb = AR.bf16(128, NTK * H * VSLOT)
        k_sb = [AR.bf16(128, S) for _ in range(2)]
        wpq = [AR.bf16(128, ND * 128) for _ in range(2)]
        wpk = [AR.bf16(128, ND * 128) for _ in range(2)]
        wpv = [AR.bf16(128, ND * 128) for _ in range(2)]
        wtmpq = [AR.f32(128, 128) for _ in range(2)]
        qstg = [AR.bf16(128, 256) for _ in range(2)]
        M_V = AR.mark()

        NW = S // 256
        pqk = PD[:, 256:512]
        pvv = PE2[:, 256:384]
        qstb = {}
        qkv_pe_done = {}
        wcast_done = {}
        wst_hist = []
        qk_evac = {}
        v_evh = {}
        hw_pe = []

        def stage_wpair(p):
            first_war = ([(pe.h, qkv_pe_done[p - 2])] if p - 2 in qkv_pe_done
                         else [])
            for mi, (wt, dstl) in enumerate(((wq, wpq), (wk, wpk), (wv, wpv))):
                for k in range(ND):
                    i = len(wst_hist)
                    wr = [wst_hist[i - 2]] if i >= 2 else []
                    bw = HQ.batch([(wtmpq[i % 2][:, :],
                                    wt[k * 128:(k + 1) * 128, p * 128:(p + 1) * 128])],
                                  wait_raw=wr)
                    first = mi == 0 and k == 0
                    P.emit("v", mk_copy(dstl[p % 2][:, k * 128:(k + 1) * 128],
                                        wtmpq[i % 2][:, :]),
                           wait=[ve],
                           wait_raw=[bw] + (first_war if first else []))
                    wst_hist.append((ve.h, ve.v))
            wcast_done[p] = (ve.h, ve.v)

        def qkv_proj(p, w, hap, which, slot, chain):
            T = w * 256
            if chain:
                P.emit("t", mk_nop(), wait_raw=list(chain))
            wsrc = wpk if which == "k" else wpq
            for k in range(ND):
                P.emit("t", mk_mm(slot, wsrc[p % 2][:, k * 128:(k + 1) * 128],
                                  hap(k), k == 0, k == ND - 1),
                       inc=([pe] if k == ND - 1 else []))
            if which == "k":
                P.emit("v", mk_ts(k_sb[p % 2][:, T:T + 256], slot,
                                  bk_s[:, p:p + 1], "add"), wait=[ve],
                       wait_raw=[(pe.h, pe.v)])
            else:
                old_q = [qstb[p][w - 2]] if w >= 2 else []
                P.emit("v", mk_ts(qstg[w % 2][:, :], slot, bq_s[:, p:p + 1], "add"),
                       wait=[ve], wait_raw=[(pe.h, pe.v)] + old_q)
            P.emit("v", mk_nop())
            qk_evac[(p, which, w)] = ve.v
            if which == "q":
                qstb.setdefault(p, []).append(
                    SQ.batch([(qT_d[:, p, T:T + 256], qstg[w % 2][:, :])], wait=[ve]))

        def qkv_v_tile(p, tk, hap_t, slot, chain):
            evh = v_evh.setdefault(p, [])
            if chain:
                P.emit("t", mk_nop(), wait_raw=list(chain))
            for k in range(ND):
                P.emit("t", mk_mm(slot, hap_t(k),
                                  wpv[p % 2][:, k * 128:(k + 1) * 128],
                                  k == 0, k == ND - 1),
                       inc=([pe] if k == ND - 1 else []))
            P.emit("v", mk_nop(), wait=[ve], wait_raw=[(pe.h, pe.v)])
            for h01 in range(2):
                hidx = 2 * p + h01
                base = (tk * H + hidx) * VSLOT
                P.emit("v", mk_tt(v_sb[:, base:base + 64],
                                  slot[:, h01 * 64:(h01 + 1) * 64],
                                  vbb[:, hidx * 64:(hidx + 1) * 64], "add"))
            P.emit("v", mk_nop())
            evh.append(ve.v)

        def emitP0():
            loads = [
                (cons, ctab[:, :]), (bq_s, bq[:, :]), (bk_s, bk[:, :]),
                (bo_s, bo[:, :]), (g2_s, gam2[:, :]), (b2_s, bet2[:, :]),
                (ffb_s, ffb[:, :]),
                (cb_s[:, 0:2], cbso[:, :]), (cb_s[:, 2:4], cbsi[:, :]),
                (gg_s[:, 0:2], gso[:, :]), (gg_s[:, 2:4], gsi[:, :]),
                (bb_s[:, 0:2], bso[:, :]), (bb_s[:, 2:4], bsi[:, :]),
                (ones_r[0:1, :], onesrow[:, :]),
            ]
            tmp = AR.f32(128, 128)
            tmpr = AR.f32(1, 2 * D)
            tmpv = AR.f32(1, D)
            loads += [(tmp[:, :], identf[:, :]), (tmpr[0:1, 0:D], lbso[:, :]),
                      (tmpr[0:1, D:2 * D], lbsi[:, :]), (tmpv[0:1, :], bvr[:, :])]
            b0 = LQ.batch(loads)
            P.emit("v", mk_copy(ident_b[:, :], tmp[:, :]), wait_raw=[b0])
            P.emit("v", mk_copy(lb_b[0:1, :], tmpr[0:1, :]))
            P.emit("v", mk_copy(ones_rb[0:1, :], ones_r[0:1, :]))
            P.emit("v", mk_copy(ones_c[:, 0:1], cons[:, 0:1]))
            P.emit("v", mk_memset(bnacc[:, :], 0.0))
            for half in range(2):
                P.emit("t", mk_mm(PA[:, half * 512:(half + 1) * 512],
                                  ones_r[0:1, :], tmpv[0:1, half * 512:(half + 1) * 512],
                                  True, True), wait=[ve], wait_raw=[b0], inc=[pe])
            P.emit("v", mk_copy(vbb[:, :], PA[:, 0:1024]), wait=[pe])
            P.emit("t", mk_nop(), wait=[ve])

        emitP0()
        barrier()

        def emitP1():
            CB = 512
            NCB = S // CB
            NT = CB // 128
            stage_wpair(0)
            stage_wpair(1)
            x_c = AR.f32(128, ND * CB)
            pe_c = AR.f32(128, ND * CB)
            xbf2 = [AR.bf16(128, ND * CB) for _ in range(2)]
            st_c = AR.f32(1, 3 * CB)
            rst_c = AR.f32(1, CB)
            vtpl = AR.bf16(128, H * VSLOT)

            P.emit("v", mk_memset(vtpl[:, :], 0.0))
            P.emit("v", mk_nop(), wait=[ve])
            for hh in range(H):
                P.emit("v", mk_memset(vtpl[:, hh * VSLOT + 64:hh * VSLOT + 65], 1.0))
            P.emit("v", mk_nop(), wait=[ve])
            for tk in range(NTK):
                P.emit("v", mk_copy(v_sb[:, tk * H * VSLOT:(tk + 1) * H * VSLOT],
                                    vtpl[:, :]))
            P.emit("v", mk_nop())

            psum_a = PA[0:1, 0:CB]
            psum_b = PA[0:1, 512:512 + CB]
            pbc = PC[:, 0:2 * CB]

            def ld_p1(c):
                t0, t1 = c * CB, (c + 1) * CB
                prs = ([(x_c[:, k * CB:(k + 1) * CB], xT[:, k, t0:t1]) for k in range(ND)]
                       + [(pe_c[:, k * CB:(k + 1) * CB], pet[:, k, t0:t1]) for k in range(ND)])
                return tuple(LQ.batch(prs[i::4], wait=[ve]) for i in range(4))

            bc = ld_p1(0)
            wv_pe = {}
            bhb_hist = {}
            for c in range(NCB):
                t0, t1 = c * CB, (c + 1) * CB
                xbf = xbf2[c % 2]
                hbf = xbf
                war = ([(pe.h, wv_pe[c - 2])] if c >= 2 else []) \
                    + ([bhb_hist[c - 2]] if c >= 2 else [])
                P.emit("v", mk_nop(), wait_raw=list(bc) + war)
                for k in range(ND):
                    P.emit("v", mk_copy(xbf[:, k * CB:(k + 1) * CB], x_c[:, k * CB:(k + 1) * CB]))
                P.emit("v", mk_nop())
                P.emit("t", mk_nop(), wait=[ve])
                for k in range(ND):
                    P.emit("t", mk_mm(psum_a, ones_c[:, 0:1], xbf[:, k * CB:(k + 1) * CB],
                                      k == 0, k == ND - 1), inc=([pe] if k == ND - 1 else []))
                P.emit("v", mk_nop(), wait=[pe])
                for k in range(ND):
                    P.emit("v", mk_tt(xbf[:, k * CB:(k + 1) * CB], xbf[:, k * CB:(k + 1) * CB],
                                      xbf[:, k * CB:(k + 1) * CB], "mult"))
                P.emit("v", mk_nop())
                P.emit("t", mk_nop(), wait=[ve])
                for k in range(ND):
                    P.emit("t", mk_mm(psum_b, ones_c[:, 0:1], xbf[:, k * CB:(k + 1) * CB],
                                      k == 0, k == ND - 1), inc=([pe] if k == ND - 1 else []))
                P.emit("v", mk_ts(st_c[0:1, 0:CB], psum_a, 1.0 / D, "mult"), wait=[pe])
                P.emit("v", mk_ts(st_c[0:1, CB:2 * CB], psum_b, 1.0 / D, "mult"))
                P.emit("v", mk_tt(rst_c[0:1, 0:CB], st_c[0:1, 0:CB], st_c[0:1, 0:CB], "mult"),
                       wait=[ve])
                P.emit("v", mk_tt(st_c[0:1, 2 * CB:3 * CB], st_c[0:1, CB:2 * CB],
                                  rst_c[0:1, 0:CB], "subtract"), wait=[ve])
                P.emit("v", mk_ts(st_c[0:1, 2 * CB:3 * CB], st_c[0:1, 2 * CB:3 * CB], EPS, "add"),
                       wait=[ve])
                P.emit("s", mk_act(rst_c[0:1, 0:CB], st_c[0:1, 2 * CB:3 * CB], AF.Ln),
                       wait=[ve])
                P.emit("s", mk_act(st_c[0:1, 2 * CB:3 * CB], rst_c[0:1, 0:CB], AF.Exp, scale=-0.5),
                       wait=[ac])
                P.emit("v", mk_tt(st_c[0:1, CB:2 * CB], st_c[0:1, 0:CB],
                                  st_c[0:1, 2 * CB:3 * CB], "mult"), wait=[ac])
                P.emit("t", mk_mm(pbc[:, 0:CB], ones_r[0:1, :], st_c[0:1, 2 * CB:3 * CB], True, True),
                       wait=[ve])
                P.emit("t", mk_mm(pbc[:, CB:2 * CB], ones_r[0:1, :], st_c[0:1, CB:2 * CB], True, True),
                       inc=[pe])
                P.emit("v", mk_nop(), wait=[pe])
                for k in range(ND):
                    P.emit("v", mk_tt(x_c[:, k * CB:(k + 1) * CB], x_c[:, k * CB:(k + 1) * CB],
                                      pbc[:, 0:CB], "mult"))
                P.emit("v", mk_nop(), wait=[ve])
                for k in range(ND):
                    P.emit("v", mk_tt(x_c[:, k * CB:(k + 1) * CB], x_c[:, k * CB:(k + 1) * CB],
                                      pbc[:, CB:2 * CB], "subtract"))
                P.emit("v", mk_nop(), wait=[ve])
                for k in range(ND):
                    P.emit("v", mk_tt(x_c[:, k * CB:(k + 1) * CB], x_c[:, k * CB:(k + 1) * CB],
                                      pe_c[:, k * CB:(k + 1) * CB], "add"))
                P.emit("v", mk_nop(), wait=[ve])
                for k in range(ND):
                    P.emit("v", mk_copy(hbf[:, k * CB:(k + 1) * CB], x_c[:, k * CB:(k + 1) * CB]))
                P.emit("v", mk_nop())
                hcast_val = ve.v
                bhb = SQ.batch([(hbf_d[:, k, t0:t1], hbf[:, k * CB:(k + 1) * CB])
                                for k in range(ND)], wait=[ve])
                bhb_hist[c] = bhb
                if c + 1 < NCB:
                    ldp = []
                    for k in range(ND):
                        bsk = SQ.batch([(hT_d[:, k, t0:t1], x_c[:, k * CB:(k + 1) * CB])],
                                       wait=([ve] if k == 0 else ()))
                        ldp.append((k, bsk))
                    lb = []
                    for k, bsk in ldp:
                        lb.append(LQ.batch(
                            [(x_c[:, k * CB:(k + 1) * CB], xT[:, k, (c + 1) * CB:(c + 2) * CB]),
                             (pe_c[:, k * CB:(k + 1) * CB], pet[:, k, (c + 1) * CB:(c + 2) * CB])],
                            wait_raw=[bsk]))
                    bc = tuple(lb)
                else:
                    SQ.batch([(hT_d[:, k, t0:t1], x_c[:, k * CB:(k + 1) * CB])
                              for k in range(ND)], wait=[ve])
                P.emit("t", mk_nop(), wait_raw=[(ve.h, hcast_val)])
                for wi in range(CB // 256):
                    w = c * (CB // 256) + wi
                    hap = (lambda wi=wi: lambda k:
                           hbf[:, k * CB + wi * 256:k * CB + wi * 256 + 256])()
                    kchain = ([wcast_done[0]] if w == 0
                              else [(ve.h, qk_evac[(0, "q", w - 1)])])
                    qkv_proj(0, w, hap, "k", pqk, kchain)
                    qkv_proj(0, w, hap, "q", pqk, [(ve.h, qk_evac[(0, "k", w)])])
                    for i2 in range(2):
                        tk = 2 * w + i2
                        hapt = (lambda wi=wi, i2=i2: lambda k:
                                hbf[:, k * CB + wi * 256 + i2 * 128:
                                    k * CB + wi * 256 + (i2 + 1) * 128])()
                        vchain = ([(ve.h, v_evh[0][tk - 1])] if tk >= 1 else [])
                        qkv_v_tile(0, tk, hapt, pvv, vchain)
                wv_pe[c] = pe.v
            qkv_pe_done[0] = pe.v
            AR.reset(M_V)

        if "p01" in phases:
            emitP1()
        barrier()

        def emitP2():
            m2 = AR.mark()
            eT = [AR.bf16(128, 2 * NTK * CH) for _ in range(3)]
            q_p = [AR.bf16(128, CH) for _ in range(4)]
            ostg = [AR.bf16(128, 2 * DH) for _ in range(2)]
            rec = AR.f32(128, 2)
            hwin = [AR.bf16(128, ND * 256) for _ in range(3)]

            pgs = [PA[:, 0:1024], PB[:, 0:1024], PC[:, 0:1024]]
            pos_ = [PD[:, 0:132], PE2[:, 0:132]]
            NGRP = NTK // 4

            hwb = {}
            lwv = [None]
            def hwin_load(g):
                T = (g % NW) * 256
                wr = [(pe.h, hw_pe[g - 3])] if g >= 3 else []
                hwb[g] = HS.batch(
                    [(hwin[g % 3].rearrange("p (k t) -> p k t", k=ND),
                      hbf_d[:, :, T:T + 256])], wait_raw=wr)

            if NP > 1:
                hwin_load(0)
                if NW > 1:
                    hwin_load(1)
            exp_hist = []
            av_hist = []
            sc_hist = []
            dv_hist = []
            dm_hist = []
            bq_pend = {}

            bq_defer = []

            def issue_bq(gcc):
                jj, cc = gcc // NCH, gcc % NCH
                if len(qstb.get(jj, ())) <= cc // 2:
                    bq_defer.append(gcc)
                    return
                war = [(pe.h, sc_hist[gcc - 4])] if gcc >= 4 else []
                bq_pend[gcc] = LQ.batch(
                    [(q_p[gcc % 4][:, :], qT_d[:, jj, cc * CH:(cc + 1) * CH])],
                    wait_raw=war + [qstb[jj][cc // 2]])

            def flush_bq():
                while bq_defer:
                    jj, cc = bq_defer[0] // NCH, bq_defer[0] % NCH
                    if len(qstb.get(jj, ())) <= cc // 2:
                        break
                    issue_bq(bq_defer.pop(0))

            for gc0 in range(min(3, NP * NCH)):
                issue_bq(gc0)
            for j in range(NP):
                if j + 2 < NP:
                    stage_wpair(j + 2)
                for c in range(NCH + 1):
                    gc = j * NCH + c
                    if c < NCH:
                        t0, t1 = c * CH, (c + 1) * CH
                        kw = ([(ve.h, qk_evac[(j, "k", NW - 1)])] if c == 0 else [])
                        P.emit("t", mk_nop(), wait_raw=[bq_pend[gc]] + kw)
                        if gc + 3 < NP * NCH:
                            issue_bq(gc + 3)
                        et = eT[gc % 3]
                        for m in range(NGRP):
                            gidx = gc * NGRP + m
                            pg = pgs[gidx % 3]
                            if gidx >= 3:
                                P.emit("t", mk_nop(), wait_raw=[(ac.h, exp_hist[gidx - 3])])
                            for q4 in range(4):
                                tk = m * 4 + q4
                                P.emit("t", mk_mm(pg[:, q4 * 128:(q4 + 1) * 128],
                                                  k_sb[j % 2][0:64, tk * 128:(tk + 1) * 128],
                                                  q_p[c % 4][0:64, :], True, True, tp=(0, 0)))
                                P.emit("t", mk_mm(pg[:, 512 + q4 * 128:512 + (q4 + 1) * 128],
                                                  k_sb[j % 2][64:128, tk * 128:(tk + 1) * 128],
                                                  q_p[c % 4][64:128, :], True, True, tp=(64, 0)),
                                       inc=([pe] if q4 == 3 else []))
                            extra = [(pe.h, av_hist[gc - 3])] if (gc >= 3 and m == 0) else []
                            P.emit("s", mk_act(et[:, m * 1024:(m + 1) * 1024], pg, AF.Exp,
                                               scale=0.125, bias=EXP_BIAS),
                                   wait=[pe], wait_raw=extra)
                            exp_hist.append(ac.v)
                            if m == NGRP - 1:
                                sc_hist.append(pe.v)
                    weave = j + 1 < NP and 1 <= c <= 2 * NW
                    if weave:
                        g = j * NW + (c - 1) // 2
                        w = g % NW
                        hv = hwin[g % 3]
                        if c % 2 == 1:
                            P.emit("t", mk_nop(), wait_raw=[hwb[g]])
                        else:
                            kchain = ([wcast_done[j + 1]] if w == 0
                                      else [(ve.h, qk_evac[(j + 1, "q", w - 1)])]) \
                                + ([(ve.h, dv_hist[-1])] if dv_hist else [])
                            qkv_proj(j + 1, w,
                                     (lambda hv=hv: lambda k:
                                      hv[:, k * 256:(k + 1) * 256])(),
                                     "k", pqk, kchain)
                    if c >= 1:
                        ca = c - 1
                        ag = j * NCH + ca
                        po = pos_[ca % 2]
                        eta = eT[ag % 3]
                        if ag >= 2:
                            P.emit("t", mk_nop(), wait_raw=[(ve.h, dv_hist[ag - 2])])
                        P.emit("t", mk_nop(), wait_raw=[(ac.h, exp_hist[ag * NGRP + NGRP - 1])]
                               + ([(ve.h, lwv[0])] if lwv[0] is not None else [])
                               + ([(ve.h, v_evh[j][-1])] if ca == 0 else []))
                        for h01 in range(2):
                            hidx = 2 * j + h01
                            ocol = h01 * VSLOT
                            for tk in range(NTK):
                                ecol = (tk // 4) * 1024 + h01 * 512 + (tk % 4) * 128
                                P.emit("t", mk_mm(po[:, ocol:ocol + 65],
                                                  eta[:, ecol:ecol + 128],
                                                  v_sb[:, (tk * H + hidx) * VSLOT:
                                                       (tk * H + hidx) * VSLOT + 65],
                                                  tk == 0, tk == NTK - 1),
                                       inc=([pe] if (h01 == 1 and tk == NTK - 1) else []))
                        av_hist.append(pe.v)
                    if weave:
                        if c % 2 == 1:
                            for tk in (2 * w, 2 * w + 1):
                                vchain = ([(ve.h, v_evh[j + 1][tk - 1])]
                                          if tk >= 1 else [])
                                if tk == 2 * w and dv_hist:
                                    vchain = vchain + [(ve.h, dv_hist[-1])]
                                qkv_v_tile(j + 1, tk,
                                           (lambda hv=hv, i2=tk - 2 * w: lambda k:
                                            hv[:, k * 256 + i2 * 128:
                                                k * 256 + (i2 + 1) * 128])(),
                                           pvv, vchain)
                        else:
                            qkv_proj(j + 1, w,
                                     (lambda hv=hv: lambda k:
                                      hv[:, k * 256:(k + 1) * 256])(),
                                     "q", pqk, [(ve.h, qk_evac[(j + 1, "k", w)])])
                            hw_pe.append(pe.v)
                            if g + 2 < (NP - 1) * NW:
                                hwin_load(g + 2)
                            if c == 2 * NW:
                                qkv_pe_done[j + 1] = pe.v
                        lwv[0] = ve.v
                        flush_bq()
                    if c >= 1:
                        ost = ostg[ca % 2]
                        P.emit("v", mk_nop(), wait=[ve], wait_raw=[(pe.h, av_hist[-1])])
                        if ag >= 2:
                            P.emit("v", mk_nop(), wait_raw=[dm_hist[ag - 2]])
                        for h01 in range(2):
                            ocol = h01 * VSLOT
                            P.emit("v", mk_recip(rec[:, h01:h01 + 1], po[:, ocol + 64:ocol + 65]))
                        P.emit("v", mk_nop(), wait=[ve])
                        for h01 in range(2):
                            ocol = h01 * VSLOT
                            P.emit("v", mk_ts(ost[:, h01 * DH:(h01 + 1) * DH],
                                              po[:, ocol:ocol + 64],
                                              rec[:, h01:h01 + 1], "mult"))
                        P.emit("v", mk_nop())
                        dv_hist.append(ve.v)
                        bo_ = SQ.batch([(on_d[ca, :, 2 * j * DH:(2 * j + 2) * DH], ost[:, :])],
                                       wait=[ve])
                        dm_hist.append(bo_)
            AR.reset(m2)

        if "p2" in phases:
            emitP2()
        barrier()

        def emitP3():
            CB = 512
            NCB = S // CB
            NT = CB // 128
            ffw_s = AR.bf16(128, ND * D)
            lw_s = AR.bf16(128, 2 * 2 * D)
            m3 = AR.mark()
            wo_s = AR.bf16(128, ND * D)
            cm_s = AR.bf16(128, 2 * ND * G)
            mW = AR.mark()
            wtmps = [AR.f32(128, D) for _ in range(2)]
            jobs = ([(wo, wo_s, k * D, D) for k in range(ND)]
                    + [(ffw, ffw_s, k * D, D) for k in range(ND)]
                    + [(cmso, cm_s, k * G, G) for k in range(ND)]
                    + [(cmsi, cm_s, 2048 + k * G, G) for k in range(ND)]
                    + [(lwso, lw_s, gc * D, D) for gc in range(2)]
                    + [(lwsi, lw_s, 2048 + gc * D, D) for gc in range(2)])
            srcrow = ([(wo, k) for k in range(ND)] + [(ffw, k) for k in range(ND)]
                      + [(cmso, k) for k in range(ND)] + [(cmsi, k) for k in range(ND)]
                      + [(lwso, gc) for gc in range(2)] + [(lwsi, gc) for gc in range(2)])
            cast_hist = []
            for i, ((wt, dst, off, w), (wt2, row)) in enumerate(zip(jobs, srcrow)):
                wr = [cast_hist[i - 2]] if i >= 2 else []
                bw = LQ.batch([(wtmps[i % 2][:, 0:w], wt[row * 128:(row + 1) * 128, :])],
                              wait_raw=wr)
                P.emit("v", mk_copy(dst[:, off:off + w], wtmps[i % 2][:, 0:w]),
                       wait_raw=[bw], inc=[ve])
                cast_hist.append((ve.h, ve.v))
            AR.reset(mW)
            onrm = AR.bf16(128, NT * D)
            oT_c = AR.bf16(128, ND * CB)
            hT_c = AR.f32(128, ND * CB)
            x_c2 = AR.f32(128, ND * CB)
            y2 = [AR.f32(128, ND * CB) for _ in range(2)]
            ybf = AR.bf16(128, ND * CB)
            st2 = AR.f32(1, 3 * CB)
            rst2 = AR.f32(1, CB)
            u2 = [AR.bf16(128, ND * CB) for _ in range(2)]
            fubf = AR.bf16(128, ND * CB)
            xcsq = AR.f32(128, 4 * CB)
            xcst = AR.f32(128, 2)

            ptrs3 = [PE2[:, 0:64].bitcast(BF16), PD[:, 128:192].bitcast(BF16)]
            pp = PD[:, 0:512]
            pst2a = PB[0:1, 0:CB]
            pst2b = PB[0:1, 512:512 + CB]
            pbc2 = PC[:, 0:2 * CB]
            pcv = PA[:, 0:CB]
            pfu = PA[:, 512:512 + CB]

            def ln_stats_bcast(srcf32, wr=()):
                P.emit("s", mk_nop(), wait=[pe], wait_raw=wr)
                for n in range(ND):
                    P.emit("s", mk_act(ybf[:, n * CB:(n + 1) * CB],
                                       srcf32[:, n * CB:(n + 1) * CB], AF.Copy))
                P.emit("s", mk_nop())
                P.emit("t", mk_nop(), wait=[ac])
                for k in range(ND):
                    P.emit("t", mk_mm(pst2a, ones_c[:, 0:1], ybf[:, k * CB:(k + 1) * CB],
                                      k == 0, k == ND - 1, skip=True),
                           inc=([pe] if k == ND - 1 else []))
                P.emit("s", mk_nop(), wait=[pe])
                for n in range(ND):
                    P.emit("s", mk_act(ybf[:, n * CB:(n + 1) * CB], ybf[:, n * CB:(n + 1) * CB],
                                       AF.Square))
                P.emit("s", mk_nop())
                P.emit("t", mk_nop(), wait=[ac])
                for k in range(ND):
                    P.emit("t", mk_mm(pst2b, ones_c[:, 0:1], ybf[:, k * CB:(k + 1) * CB],
                                      k == 0, k == ND - 1, skip=True),
                           inc=([pe] if k == ND - 1 else []))
                P.emit("v", mk_ts(st2[0:1, 0:CB], pst2a, 1.0 / D, "mult"), wait=[pe])
                P.emit("v", mk_ts(st2[0:1, CB:2 * CB], pst2b, 1.0 / D, "mult"))
                P.emit("v", mk_tt(rst2[0:1, 0:CB], st2[0:1, 0:CB], st2[0:1, 0:CB], "mult"),
                       wait=[ve])
                P.emit("v", mk_tt(st2[0:1, 2 * CB:3 * CB], st2[0:1, CB:2 * CB],
                                  rst2[0:1, 0:CB], "subtract"), wait=[ve])
                P.emit("v", mk_ts(st2[0:1, 2 * CB:3 * CB], st2[0:1, 2 * CB:3 * CB], EPS, "add"),
                       wait=[ve])
                P.emit("s", mk_act(rst2[0:1, 0:CB], st2[0:1, 2 * CB:3 * CB], AF.Ln), wait=[ve])
                P.emit("s", mk_act(st2[0:1, 2 * CB:3 * CB], rst2[0:1, 0:CB], AF.Exp, scale=-0.5),
                       wait=[ac])
                P.emit("v", mk_tt(st2[0:1, CB:2 * CB], st2[0:1, 0:CB],
                                  st2[0:1, 2 * CB:3 * CB], "mult"), wait=[ac])
                P.emit("t", mk_mm(pbc2[:, 0:CB], ones_r[0:1, :], st2[0:1, 2 * CB:3 * CB],
                                  True, True), wait=[ve])
                P.emit("t", mk_mm(pbc2[:, CB:2 * CB], ones_r[0:1, :], st2[0:1, CB:2 * CB],
                                  True, True), inc=[pe])

            def ld_p3(c, w=(), wr=()):
                t0, t1 = c * CB, (c + 1) * CB
                prs = ([(onrm[:, i * D:(i + 1) * D], on_d[c * NT + i, :, :]) for i in range(NT)]
                       + [(hT_c[:, k * CB:(k + 1) * CB], hT_d[:, k, t0:t1]) for k in range(ND)]
                       + [(x_c2[:, k * CB:(k + 1) * CB], xT[:, k, t0:t1]) for k in range(ND)])
                return tuple(LQ.batch(prs[i::4], wait=w, wait_raw=wr) for i in range(4))

            bc3 = ld_p3(0, w=[pe, ve])
            sttv = {}
            bx1h = {}
            bxh = {}
            for c in range(NCB + 2):
              if c < NCB:
                t0, t1 = c * CB, (c + 1) * CB
                P.emit("t", mk_nop(), wait=[ve], wait_raw=list(bc3))
                trh = []
                for i in range(NT):
                    for k in range(ND):
                        idx = i * ND + k
                        if idx >= 2:
                            P.emit("t", mk_nop(), wait_raw=[(ve.h, trh[idx - 2])])
                        P.emit("t", mk_trans(ptrs3[idx % 2],
                                             onrm[:, i * D + k * 128:i * D + (k + 1) * 128],
                                             ident_b[:, :]), inc=[pe])
                        P.emit("v", mk_copy(oT_c[:, k * CB + i * 128:k * CB + (i + 1) * 128],
                                            ptrs3[idx % 2]), wait=[pe])
                        trh.append(ve.v)
                b_on = None
                if c + 1 < NCB:
                    prs_on = [(onrm[:, i * D:(i + 1) * D], on_d[(c + 1) * NT + i, :, :])
                              for i in range(NT)]
                    b_on = [LQ.batch(prs_on[i::2], wait_raw=[(pe.h, pe.v)])
                            for i in range(2)]
                P.emit("t", mk_nop(), wait=[ve])
                yb = y2[c % 2]
                for n in range(ND):
                    for k in range(ND):
                        P.emit("t", mk_mm(pp, wo_s[:, k * D + n * 128:k * D + (n + 1) * 128],
                                          oT_c[:, k * CB:(k + 1) * CB], k == 0, k == ND - 1),
                               inc=([pe] if k == ND - 1 else []))
                    P.emit("v", mk_stt(yb[:, n * CB:(n + 1) * CB], pp, bo_s[:, n:n + 1],
                                       hT_c[:, n * CB:(n + 1) * CB], "add", "add"), wait=[pe])
                    P.emit("t", mk_nop(), wait=[ve])
                sttv[c] = ve.v
                b_h3 = None
                if c + 1 < NCB:
                    prs_h3 = [(hT_c[:, k * CB:(k + 1) * CB],
                               hT_d[:, k, (c + 1) * CB:(c + 2) * CB])
                              for k in range(ND)]
                    b_h3 = [LQ.batch(prs_h3[i::2], wait=[ve]) for i in range(2)]
                if c + 1 < NCB:
                    bc3 = tuple(b_on) + tuple(b_h3)
              if 1 <= c <= NCB:
                cl = c - 1
                tl0, tl1 = cl * CB, (cl + 1) * CB
                yb = y2[cl % 2]
                ln_stats_bcast(yb, wr=[(ve.h, sttv[cl])])
                P.emit("v", mk_nop(), wait=[pe], wait_raw=[(ve.h, sttv[cl])]
                       + ([bxh[cl]] if cl in bxh else []))
                for n in range(ND):
                    P.emit("v", mk_tt(yb[:, n * CB:(n + 1) * CB], yb[:, n * CB:(n + 1) * CB],
                                      pbc2[:, 0:CB], "mult"))
                P.emit("v", mk_nop(), wait=[ve])
                for n in range(ND):
                    P.emit("v", mk_tt(yb[:, n * CB:(n + 1) * CB], yb[:, n * CB:(n + 1) * CB],
                                      pbc2[:, CB:2 * CB], "subtract"))
                P.emit("v", mk_nop(), wait=[ve])
                for n in range(ND):
                    P.emit("v", mk_ts2(yb[:, n * CB:(n + 1) * CB], yb[:, n * CB:(n + 1) * CB],
                                       g2_s[:, n:n + 1], "mult", b2_s[:, n:n + 1], "add"))
                P.emit("v", mk_nop(), wait=[ve])
                for n in range(ND):
                    P.emit("v", mk_tt(yb[:, n * CB:(n + 1) * CB], yb[:, n * CB:(n + 1) * CB],
                                      x_c2[:, n * CB:(n + 1) * CB], "add"))
                P.emit("v", mk_nop())
                bx1 = SQ.batch([(x1_d[:, k, tl0:tl1], yb[:, k * CB:(k + 1) * CB])
                                for k in range(ND)], wait=[ve])
                bx1h[cl] = bx1
                if cl + 1 < NCB:
                    bxh[cl + 1] = LQ.batch([(x_c2[:, k * CB:(k + 1) * CB],
                                             xT[:, k, (cl + 1) * CB:(cl + 2) * CB])
                                            for k in range(ND)], wait=[ve])
                ln_stats_bcast(yb, wr=[(ve.h, ve.v)])
                ub = u2[cl % 2]
                P.emit("v", mk_nop(), wait=[pe], wait_raw=[bx1])
                for n in range(ND):
                    P.emit("v", mk_tt(yb[:, n * CB:(n + 1) * CB], yb[:, n * CB:(n + 1) * CB],
                                      pbc2[:, 0:CB], "mult"))
                P.emit("v", mk_nop(), wait=[ve])
                for n in range(ND):
                    P.emit("v", mk_tt(ub[:, n * CB:(n + 1) * CB], yb[:, n * CB:(n + 1) * CB],
                                      pbc2[:, CB:2 * CB], "subtract"))
                P.emit("v", mk_nop())
                SQ.batch([(uT_d[:, k, tl0:tl1], ub[:, k * CB:(k + 1) * CB])
                          for k in range(ND)], wait=[ve])
              if 2 <= c:
                cf = c - 2
                tf0, tf1 = cf * CB, (cf + 1) * CB
                uf = u2[cf % 2]
                bxc = {}
                P.emit("t", mk_nop(), wait=[ve])
                for br in range(2):
                    for gc in range(2):
                        for k in range(ND):
                            P.emit("t", mk_mm(pcv, cm_s[:, br * 2048 + k * G + gc * 128:
                                                        br * 2048 + k * G + (gc + 1) * 128],
                                              uf[:, k * CB:(k + 1) * CB], k == 0, k == ND - 1),
                                   inc=([pe] if k == ND - 1 else []))
                        col = 2 * br + gc
                        P.emit("v", mk_ts(xcsq[:, col * CB:(col + 1) * CB], pcv,
                                          cb_s[:, col:col + 1], "add"), wait=[pe])
                        P.emit("t", mk_nop(), wait=[ve])
                        bxc[col] = SQ.batch([(xc_d[:, col, tf0:tf1],
                                              xcsq[:, col * CB:(col + 1) * CB])], wait=[ve])
                P.emit("v", mk_nop(), wait=[ve])
                for col in range(4):
                    P.emit("v", mk_red(xcst[:, 0:1], xcsq[:, col * CB:(col + 1) * CB], "add"))
                    P.emit("v", mk_tt(bnacc[:, col:col + 1], bnacc[:, col:col + 1],
                                      xcst[:, 0:1], "add"), wait=[ve])
                    P.emit("v", mk_tt(xcsq[:, col * CB:(col + 1) * CB],
                                      xcsq[:, col * CB:(col + 1) * CB],
                                      xcsq[:, col * CB:(col + 1) * CB], "mult"),
                           wait=[ve], wait_raw=[bxc[col]])
                    P.emit("v", mk_red(xcst[:, 1:2], xcsq[:, col * CB:(col + 1) * CB], "add"),
                           wait=[ve])
                    P.emit("v", mk_tt(bnacc[:, 4 + col:5 + col], bnacc[:, 4 + col:5 + col],
                                      xcst[:, 1:2], "add"), wait=[ve])
                P.emit("t", mk_nop(), wait=[ve])
                for n in range(ND):
                    for k in range(ND):
                        P.emit("t", mk_mm(pfu, ffw_s[:, k * D + n * 128:k * D + (n + 1) * 128],
                                          uf[:, k * CB:(k + 1) * CB], k == 0, k == ND - 1),
                               inc=([pe] if k == ND - 1 else []))
                    P.emit("v", mk_copy(fubf[:, n * CB:(n + 1) * CB], pfu), wait=[pe])
                    P.emit("t", mk_nop(), wait=[ve])
                SQ.batch([(fu_d[:, k, tf0:tf1], fubf[:, k * CB:(k + 1) * CB])
                          for k in range(ND)], wait=[ve])
            AR.reset(m3)
            return ffw_s, lw_s

        AR.reset(M_PHASE)
        if "p3" in phases:
            ffw_s, lw_s = emitP3()
        else:
            ffw_s = AR.bf16(128, ND * D); lw_s = AR.bf16(128, 2 * 2 * D)
        barrier()

        def emitCC():
            b1 = SQ.batch([(cc_i[:, :], bnacc[:, :])])
            P.emit("g", mk_nop(), wait_raw=[b1])
            P.emit("g", mk_coll(cc_i[:, :], cc_o[:, :], ccs.h), noauto=True)
            ccs.v = 1
            P.emit("g", mk_nop(), wait_raw=[(ccs.h, 1)])
            b2 = SQ.batch([(bnacc[:, :], cc_o[:, :])])
            P.emit("v", mk_nop(), wait_raw=[b2])
            tmp = AR.f32(128, 4)
            for col in range(4):
                P.emit("v", mk_ts(tmp[:, 0:1], bnacc[:, col:col + 1], 1.0 / 32768.0, "mult"))
                P.emit("v", mk_ts(tmp[:, 1:2], bnacc[:, 4 + col:5 + col], 1.0 / 32768.0, "mult"),
                       wait=[ve])
                P.emit("v", mk_tt(tmp[:, 2:3], tmp[:, 0:1], tmp[:, 0:1], "mult"),
                       wait=[ve])
                P.emit("v", mk_tt(tmp[:, 1:2], tmp[:, 1:2], tmp[:, 2:3], "subtract"),
                       wait=[ve])
                P.emit("v", mk_ts(tmp[:, 1:2], tmp[:, 1:2], EPS, "add"), wait=[ve])
                P.emit("s", mk_act(tmp[:, 2:3], tmp[:, 1:2], AF.Ln), wait=[ve])
                P.emit("s", mk_act(tmp[:, 3:4], tmp[:, 2:3], AF.Exp, scale=-0.5),
                       wait=[ac])
                P.emit("v", mk_tt(bnp_sc[:, col:col + 1], tmp[:, 3:4], gg_s[:, col:col + 1],
                                  "mult"), wait=[ac])
                P.emit("v", mk_tt(tmp[:, 0:1], tmp[:, 0:1], bnp_sc[:, col:col + 1], "mult"),
                       wait=[ve])
                P.emit("v", mk_tt(bnp_bi[:, col:col + 1], bb_s[:, col:col + 1], tmp[:, 0:1],
                                  "subtract"), wait=[ve])

        if "cc" in phases and "fb" not in phases:
            emitCC()
            barrier()

        def emitFB():
            CB = 512
            NCB = S // CB
            NT = CB // 128
            NTL = 2 * NT
            m4 = AR.mark()
            xc_c = AR.f32(128, 4 * CB)
            fu_c = AR.bf16(128, ND * CB)
            u_c = AR.bf16(128, ND * CB)
            x1_c = AR.f32(128, ND * CB)
            xa = AR.bf16(128, 4 * CB)
            e_s = [AR.f32(128, D) for _ in range(NTL)]
            z0 = [AR.f32(128, D) for _ in range(NTL)]
            m8 = AR.f32(128, 8 * NTL)
            den = AR.f32(128, 2 * NTL)
            gbf = [AR.bf16(128, D) for _ in range(2)]
            gT = [AR.bf16(128, ND * CB) for _ in range(2)]
            ug = AR.bf16(128, ND * CB)
            tmb = AR.f32(128, ND * CB)
            yo = tmb

            pLs = [PA[:, 0:1024], PC[:, 0:1024]]
            ptrs = [PE2[:, 0:64].bitcast(BF16), PD[:, 0:64].bitcast(BF16)]
            pI = [PB[:, 0:CB], PB[:, 512:512 + CB]]

            def ld_fb(tn, w=(), wr=()):
                prs = ([(xc_c.rearrange("p (q t) -> p q t", q=4), xc_d[:, :, tn:tn + CB])]
                       + [(fu_c[:, k * CB:(k + 1) * CB], fu_d[:, k, tn:tn + CB])
                          for k in range(ND)]
                       + [(u_c[:, k * CB:(k + 1) * CB], uT_d[:, k, tn:tn + CB])
                          for k in range(ND)])
                return [LQ.batch(prs[i::4], wait=w, wait_raw=wr) for i in range(4)]

            bfb = ld_fb(0, w=[ve])
            bx1b = [LQ.batch([(x1_c[:, k * CB:(k + 1) * CB], x1_d[:, k, 0:CB])
                              for k in range(ND)][i::2], wait=[ve]) for i in range(2)]
            if "cc" in phases:
                emitCC()
            barrier()
            exp_done = []
            sub_done = {}
            xa_free = {}
            gelu_ac = {}
            for c in range(NCB):
                t0, t1 = c * CB, (c + 1) * CB
                P.emit("s", mk_nop(),
                       wait_raw=list(bfb) + ([(pe.h, xa_free[c - 1])] if c else []))
                for col in range(4):
                    P.emit("s", mk_act(xa[:, col * CB:(col + 1) * CB],
                                       xc_c[:, col * CB:(col + 1) * CB], AF.Gelu,
                                       scale=bnp_sc[:, col:col + 1],
                                       bias=bnp_bi[:, col:col + 1]))
                P.emit("s", mk_nop())
                gelu_ac[c] = ac.v
                P.emit("t", mk_nop(), wait_raw=[(ac.h, gelu_ac[c])])
                for ti in range(NTL):
                    br, tt = ti // NT, ti % NT
                    pL = pLs[ti % 2]
                    gti = c * NTL + ti
                    if gti >= 2:
                        P.emit("t", mk_nop(), wait_raw=[(ac.h, exp_done[gti - 2])])
                    for half in range(2):
                        for gc in range(2):
                            col = 2 * br + gc
                            P.emit("t", mk_mm(pL[:, half * 512:(half + 1) * 512],
                                              xa[:, col * CB + tt * 128:col * CB + (tt + 1) * 128],
                                              lw_s[:, br * 2048 + gc * D + half * 512:
                                                   br * 2048 + gc * D + (half + 1) * 512],
                                              gc == 0, False))
                        P.emit("t", mk_mm(pL[:, half * 512:(half + 1) * 512],
                                          ones_rb[0:1, :],
                                          lb_b[0:1, br * D + half * 512:br * D + (half + 1) * 512],
                                          False, True), inc=([pe] if half == 1 else []))
                    if ti == NTL - 1:
                        xa_free[c] = pe.v
                    P.emit("s", mk_act(e_s[ti][:, :], pL, AF.Exp), wait=[pe],
                           wait_raw=([(ve.h, sub_done[(c - 1, ti)])] if c else []))
                    exp_done.append(ac.v)
                P.emit("v", mk_nop(), wait=[ac])
                for ti in range(NTL):
                    P.emit("v", mk_red(den[:, 2 * ti:2 * ti + 1], e_s[ti][:, :], "add"))
                P.emit("v", mk_nop(), wait=[ve])
                for ti in range(NTL):
                    P.emit("v", mk_recip(den[:, 2 * ti + 1:2 * ti + 2],
                                         den[:, 2 * ti:2 * ti + 1]))
                for r in range(4):
                    P.emit("v", mk_nop(), wait=[ve])
                    for ti in range(NTL):
                        srcz = e_s[ti] if r == 0 else z0[ti]
                        P.emit("v", mk_max8(m8[:, 8 * ti:8 * (ti + 1)], srcz[:, :]))
                    P.emit("v", mk_nop(), wait=[ve])
                    for ti in range(NTL):
                        srcz = e_s[ti] if r == 0 else z0[ti]
                        P.emit("v", mk_mr(z0[ti][:, :], m8[:, 8 * ti:8 * (ti + 1)],
                                          srcz[:, :], 0.0))
                P.emit("v", mk_nop(), wait=[ve])
                tr_hist = []
                for ti in range(NTL):
                    br, tt = ti // NT, ti % NT
                    P.emit("v", mk_tt(e_s[ti][:, :], e_s[ti][:, :], z0[ti][:, :], "subtract"),
                           wait=([ve] if ti == 0 else ()))
                    gb = gbf[ti % 2]
                    if ti >= 2:
                        P.emit("v", mk_nop(), wait_raw=[(pe.h, tr_hist[(ti - 2) * ND + ND - 1][0])])
                    P.emit("v", mk_ts(gb[:, :], e_s[ti][:, :],
                                      den[:, 2 * ti + 1:2 * ti + 2], "mult"), wait=[ve])
                    sub_done[(c, ti)] = ve.v
                    for k in range(ND):
                        i = ti * ND + k
                        if i >= 2:
                            P.emit("t", mk_nop(), wait_raw=[(ac.h, tr_hist[i - 2][1])])
                        P.emit("t", mk_trans(ptrs[i % 2], gb[:, k * 128:(k + 1) * 128],
                                             ident_b[:, :]),
                               wait=([ve] if k == 0 else ()), inc=[pe])
                        P.emit("s", mk_act(gT[br][:, k * CB + tt * 128:k * CB + (tt + 1) * 128],
                                           ptrs[i % 2], AF.Copy), wait=[pe])
                        tr_hist.append((pe.v, ac.v))
                P.emit("v", mk_nop(), wait=[ve, pe, ac])
                for k in range(ND):
                    P.emit("v", mk_tt(ug[:, k * CB:(k + 1) * CB], u_c[:, k * CB:(k + 1) * CB],
                                      gT[1][:, k * CB:(k + 1) * CB], "mult"))
                P.emit("v", mk_nop())
                for n in range(ND):
                    P.emit("v", mk_tt(tmb[:, n * CB:(n + 1) * CB], fu_c[:, n * CB:(n + 1) * CB],
                                      gT[0][:, n * CB:(n + 1) * CB], "mult"),
                           wait=([ve] if n == 0 else ()))
                P.emit("v", mk_nop())
                if c + 1 < NCB:
                    bfb = ld_fb((c + 1) * CB, w=[ve])
                P.emit("t", mk_nop(), wait=[ve])
                for n in range(ND):
                    pi = pI[n % 2]
                    for k in range(ND):
                        P.emit("t", mk_mm(pi, ffw_s[:, k * D + n * 128:k * D + (n + 1) * 128],
                                          ug[:, k * CB:(k + 1) * CB], k == 0, k == ND - 1),
                               inc=([pe] if k == ND - 1 else []))
                    P.emit("v", mk_tt(tmb[:, n * CB:(n + 1) * CB], tmb[:, n * CB:(n + 1) * CB],
                                      pi, "add"), wait=[pe, ve])
                    P.emit("t", mk_nop(), wait=[ve])
                    P.emit("v", mk_stt(yo[:, n * CB:(n + 1) * CB], tmb[:, n * CB:(n + 1) * CB],
                                       ffb_s[:, n:n + 1], x1_c[:, n * CB:(n + 1) * CB],
                                       "add", "add"), wait=[ve],
                           wait_raw=(list(bx1b) if n == 0 else ()))
                P.emit("g", mk_nop(), wait=[ve])
                SQ.batch([(yout[:, k, t0:t1], yo[:, k * CB:(k + 1) * CB])
                          for k in range(ND)], wait=[ve])
                if c + 1 < NCB:
                    tn = (c + 1) * CB
                    bx1b = [LQ.batch([(x1_c[:, k * CB:(k + 1) * CB], x1_d[:, k, tn:tn + CB])
                                      for k in range(ND)][i::2], wait=[ve])
                            for i in range(2)]
                P.emit("v", mk_nop(), wait=SQA)
            AR.reset(m4)

        if "fb" in phases:
            emitFB()
        barrier()
        P.emit("g", mk_nop(), wait=SQA + LQA)

        with nc.Block() as block:
            block.tensor(lambda T: [f(T) for f in P.ops["t"]])
            block.vector(lambda V: [f(V) for f in P.ops["v"]])
            block.scalar(lambda A: [f(A) for f in P.ops["s"]])
            block.gpsimd(lambda GP: [f(GP) for f in P.ops["g"]])
            block.sync(lambda SY: [f(SY) for f in P.ops["y"]])
    return nc



def _pe_table():
    pos = np.arange(S, dtype=np.float64)[:, None]
    div = np.exp(np.arange(0, D, 2, dtype=np.float64) * (-math.log(10000.0) / D))
    pe_t = np.zeros((S, D), dtype=np.float32)
    pe_t[:, 0::2] = np.sin(pos * div).astype(np.float32)
    pe_t[:, 1::2] = np.cos(pos * div).astype(np.float32)
    return pe_t


def _feat_major(a2d):
    return np.ascontiguousarray(a2d.T.reshape(ND, 128, S).transpose(1, 0, 2))


def _col_major_bias(b):
    return np.ascontiguousarray(b.reshape(ND, 128).T)


def _embed_conv(cw):
    cm = np.zeros((D, G), dtype=np.float32)
    g = np.arange(G)
    for f in range(4):
        cm[g * 4 + f, g] = cw[:, f]
    return cm


def kernel(**inp):
    x = np.asarray(inp["L0_attn_x"], dtype=np.float32)
    pe_t = _pe_table()

    def f32(v):
        return np.ascontiguousarray(np.asarray(v, dtype=np.float32))

    ctab = np.zeros((128, 16), np.float32)
    for col, val in {0: 1.0, 1: 0.125, 2: EXP_BIAS, 3: 1.0 / D, 4: -0.5, 5: EPS,
                     6: 1.0 / 32768.0, 7: 0.0, 8: -1.0}.items():
        ctab[:, col] = val

    shared = {
        "pet": _feat_major(pe_t),
        "wq": f32(inp["L0_attn_Wq"]), "wk": f32(inp["L0_attn_Wk"]),
        "wv": f32(inp["L0_attn_Wv"]), "wo": f32(inp["L0_attn_Wo"]),
        "bq": _col_major_bias(f32(inp["L0_attn_bq"])),
        "bk": _col_major_bias(f32(inp["L0_attn_bk"])),
        "bvr": f32(inp["L0_attn_bv"])[None, :],
        "bo": _col_major_bias(f32(inp["L0_attn_bo"])),
        "gam2": _col_major_bias(f32(inp["L0_attn_gamma"])),
        "bet2": _col_major_bias(f32(inp["L0_attn_beta"])),
        "ffw": f32(inp["L0_ffn_weight"]),
        "ffb": _col_major_bias(f32(inp["L0_ffn_bias"])),
        "cmso": _embed_conv(f32(inp["L0_ffn_so_conv_w"])),
        "cmsi": _embed_conv(f32(inp["L0_ffn_si_conv_w"])),
        "cbso": np.ascontiguousarray(f32(inp["L0_ffn_so_conv_b"]).reshape(2, 128).T),
        "cbsi": np.ascontiguousarray(f32(inp["L0_ffn_si_conv_b"]).reshape(2, 128).T),
        "gso": np.ascontiguousarray(f32(inp["L0_ffn_so_gamma"]).reshape(2, 128).T),
        "bso": np.ascontiguousarray(f32(inp["L0_ffn_so_beta"]).reshape(2, 128).T),
        "gsi": np.ascontiguousarray(f32(inp["L0_ffn_si_gamma"]).reshape(2, 128).T),
        "bsi": np.ascontiguousarray(f32(inp["L0_ffn_si_beta"]).reshape(2, 128).T),
        "lwso": np.ascontiguousarray(f32(inp["L0_ffn_so_lin_w"]).T),
        "lwsi": np.ascontiguousarray(f32(inp["L0_ffn_si_lin_w"]).T),
        "lbso": f32(inp["L0_ffn_so_lin_b"])[None, :],
        "lbsi": f32(inp["L0_ffn_si_lin_b"])[None, :],
        "identf": np.eye(128, dtype=np.float32),
        "ctab": ctab,
        "onesrow": np.ones((1, 128), np.float32),
    }

    nc = build_nc()
    in_maps = []
    for i in range(NCORES):
        m = dict(shared)
        m["xT"] = _feat_major(x[i])
        in_maps.append(m)
    res = run_bass_kernel_spmd(nc, in_maps, core_ids=list(range(NCORES)))
    outs = []
    for i in range(NCORES):
        y = np.asarray(res.results[i]["yout"])
        outs.append(y.transpose(1, 0, 2).reshape(D, S).T)
    return np.stack(outs, axis=0)


if __name__ == "__main__":
    rng = np.random.default_rng(0)
    print("building program ...")
    nc = build_nc()
    print("built")


# version 26 (final)
# speedup vs baseline: 1.0374x; 1.0293x over previous
import math
from contextlib import ExitStack

import numpy as np
import concourse.bass as bass
import concourse.mybir as mybir
from concourse.bass_utils import run_bass_kernel_spmd

F32 = mybir.dt.float32
BF16 = mybir.dt.bfloat16
AF = mybir.ActivationFunctionType
ALU = mybir.AluOpType
AX = mybir.AxisListType

S = 4096
D = 1024
H = 16
DH = 64
G = 256
ND = 8
NP = 8
CH = 128
NCH = S // CH
NTK = S // 128
VSLOT = 66
EXP_BIAS = -10.0
EPS = 1e-5
NCORES = 8

ARENA_COLS = 49152


class Sem:
    def __init__(self, ctx, nc, name):
        self.h = ctx.enter_context(nc.semaphore(name))
        self.v = 0


class Arena:

    def __init__(self, h):
        self.h = h
        self.top = 0

    def f32(self, rows, cols):
        o = self.top
        self.top += cols
        assert self.top <= ARENA_COLS, f"SBUF arena overflow: {self.top}"
        return self.h[0:rows, o:o + cols]

    def bf16(self, rows, cols):
        o = self.top
        self.top += (cols + 1) // 2
        assert self.top <= ARENA_COLS, f"SBUF arena overflow: {self.top}"
        return self.h[0:128, o:o + (cols + 1) // 2].bitcast(BF16)[0:rows, 0:cols]

    def mark(self):
        return self.top

    def reset(self, m):
        self.top = m


class Prog:

    KEYS = ("t", "v", "s", "g", "y")
    DMA_DEPTH = 8

    def __init__(self):
        self.ops = {k: [] for k in self.KEYS}
        self.dmacnt = {}

    def set_dma_sems(self, sems):
        self.dma_hs = {id(s.h) for s in sems}
        self.dma_h2 = {s.h for s in sems}

    def __init_sems__(self, esem):
        self.esem = esem
        self.bneed = set()

    def emit(self, key, fn, wait=(), inc=(), inc16=(), wait_raw=(), noauto=False):
        waits = tuple((s.h, s.v) for s in wait) + tuple(wait_raw)
        for s in wait:
            if s in self.esem.values():
                self.bneed.add(s)
        own = self.esem.get(key)
        auto = key in ("v", "s", "g") and not noauto
        if own is not None and own in self.bneed:
            if not waits:
                waits = ((own.h, own.v),)
            self.bneed.discard(own)
        step = 16 if key in ("g", "y") else 1
        pre = tuple((s.h, s.v) for s in inc)
        incs = tuple([(s.h, step) for s in inc] + [(s.h, 16) for s in inc16])
        if auto and not incs:
            incs = ((own.h, step),)
            own.v += step
        if incs and not inc16:
            dma_hs = getattr(self, "dma_hs", set())
            if not any(id(h) not in dma_hs for h, v in waits):
                sw = (own.h, own.v - (step if (auto or own in inc) else 0)) \
                    if own is not None else (pre[0] if pre else (incs[0][0], 0))
                waits = waits + (sw,)
        for s in inc:
            s.v += step
        for s in inc16:
            s.v += 16

        def cl(eng, fn=fn, waits=waits, incs=incs):
            for h, v in waits:
                eng.wait_ge(h, v)
            inst = fn(eng)
            for h, n in incs:
                inst.then_inc(h, n)

        self.ops[key].append(cl)


class DmaQ:

    def __init__(self, P, key, sems):
        self.P, self.key, self.sems, self.i = P, key, sems, 0

    def batch(self, pairs, wait=(), wait_raw=()):
        s = self.sems[self.i % len(self.sems)]
        self.i += 1
        pairs = list(pairs)
        for n, (dst, srcap) in enumerate(pairs):
            self.P.emit(self.key, mk_dma(dst, srcap),
                        wait=(tuple(wait) + (s,)) if n == 0 else (),
                        wait_raw=wait_raw if n == 0 else (), inc16=[s])
        return (s.h, s.v)



def mk_nop():
    return lambda e: e.nop()


def mk_dma(dst, src):
    return lambda e: e.dma_start(out=dst, in_=src)


def mk_mm(o, a, b, st, sp, tp=None, skip=False):
    if tp is None:
        return lambda T: T.matmul(o, a, b, start=st, stop=sp, skip_group_check=skip)
    return lambda T: T.matmul(o, a, b, start=st, stop=sp, tile_position=tp,
                              skip_group_check=skip)


def mk_trans(o, i, ident):
    return lambda T: T.transpose(o, i, ident)


def mk_act(o, i, func, scale=None, bias=None):
    kw = {}
    if scale is not None:
        kw["scale"] = scale
    if bias is not None:
        kw["bias"] = bias
    return lambda A: A.activation(out=o, in_=i, func=func, **kw)


def mk_copy(o, i):
    return lambda V: V.tensor_copy(out=o, in_=i)


def mk_tt(o, a, b, opname):
    op = getattr(ALU, opname)
    return lambda V: V.tensor_tensor(out=o, in0=a, in1=b, op=op)


def mk_ts(o, a, s1, opname):
    op = getattr(ALU, opname)
    return lambda V: V.tensor_scalar(out=o, in0=a, scalar1=s1, scalar2=None, op0=op)


def mk_ts2(o, a, s1, op0name, s2, op1name):
    op0 = getattr(ALU, op0name)
    op1 = getattr(ALU, op1name)
    return lambda V: V.tensor_scalar(out=o, in0=a, scalar1=s1, scalar2=s2,
                                     op0=op0, op1=op1)


def mk_stt(o, a, s1, b, op0name, op1name):
    op0 = getattr(ALU, op0name)
    op1 = getattr(ALU, op1name)
    return lambda V: V.scalar_tensor_tensor(out=o, in0=a, scalar=s1, in1=b,
                                            op0=op0, op1=op1)


def mk_red(o, i, opname):
    op = getattr(ALU, opname)
    return lambda V: V.tensor_reduce(out=o, in_=i, axis=AX.X, op=op)


def mk_recip(o, i):
    return lambda V: V.reciprocal(out=o, in_=i)


def mk_max8(o, i):
    return lambda V: V.max(out=o, in_=i)


def mk_mr(o, m8, vals, imm):
    return lambda V: V.match_replace(out=o, in_to_replace=m8, in_values=vals,
                                     imm_value=imm)


def mk_memset(ap, val):
    return lambda V: V.memset(ap, val)


def mk_coll(ins, outs, sem):
    def f(GP):
        i = GP.collective_compute(
            "AllReduce", ALU.add, replica_groups=[list(range(NCORES))],
            ins=[ins], outs=[outs])
        i.then_inc(sem)
        return _NoInc(i)
    return f


class _NoInc:
    def __init__(self, i):
        self.i = i

    def then_inc(self, h, n):
        raise AssertionError("collective inc is managed internally")



def build_nc(debug_out=(), phases=("p01", "p2", "p3", "cc", "fb")):
    nc = bass.Bass("TRN2", target_bir_lowering=False, debug=False)

    def din(name, shape, dt=F32):
        return nc.dram_tensor(name, list(shape), dt, kind="ExternalInput").ap()

    xT = din("xT", (128, ND, S))
    pet = din("pet", (128, ND, S))
    wq = din("wq", (D, D))
    wk = din("wk", (D, D))
    wv = din("wv", (D, D))
    wo = din("wo", (D, D))
    bq = din("bq", (128, ND))
    bk = din("bk", (128, ND))
    bvr = din("bvr", (1, D))
    bo = din("bo", (128, ND))
    gam2 = din("gam2", (128, ND))
    bet2 = din("bet2", (128, ND))
    ffw = din("ffw", (D, D))
    ffb = din("ffb", (128, ND))
    cmso = din("cmso", (D, G))
    cmsi = din("cmsi", (D, G))
    cbso = din("cbso", (128, 2))
    cbsi = din("cbsi", (128, 2))
    gso = din("gso", (128, 2))
    bso = din("bso", (128, 2))
    gsi = din("gsi", (128, 2))
    bsi = din("bsi", (128, 2))
    lwso = din("lwso", (G, D))
    lwsi = din("lwsi", (G, D))
    lbso = din("lbso", (1, D))
    lbsi = din("lbsi", (1, D))
    identf = din("identf", (128, 128))
    ctab = din("ctab", (128, 16))
    onesrow = din("onesrow", (1, 128))

    yout = nc.dram_tensor("yout", [128, ND, S], F32, kind="ExternalOutput").ap()

    def scratch(name, shape, dt):
        kind = "ExternalOutput" if name in debug_out else "Internal"
        return nc.dram_tensor(name, list(shape), dt, kind=kind).ap()

    hT_d = scratch("hT_d", (128, ND, S), F32)
    qT_d = scratch("qT_d", (128, ND, S), BF16)
    hbf_d = scratch("hbf_d", (128, ND, S), BF16)
    on_d = scratch("on_d", (NCH, 128, D), BF16)
    x1_d = scratch("x1_d", (128, ND, S), F32)
    uT_d = scratch("uT_d", (128, ND, S), BF16)
    xc_d = scratch("xc_d", (128, 4, S), F32)
    fu_d = scratch("fu_d", (128, ND, S), BF16)
    cc_i = nc.dram_tensor("cc_i", [128, 8], F32, kind="Internal").ap()
    cc_o = nc.dram_tensor("cc_o", [128, 8], F32, kind="Internal").ap()

    P = Prog()
    est = ExitStack()
    with est:
        arena_t = est.enter_context(nc.sbuf_tensor("arena", [128, ARENA_COLS], F32))
        AR = Arena(arena_t)

        PA = est.enter_context(nc.psum_tensor("PA", [128, 1024], F32))
        PB = est.enter_context(nc.psum_tensor("PB", [128, 1024], F32))
        PC = est.enter_context(nc.psum_tensor("PC", [128, 1024], F32))
        PD = est.enter_context(nc.psum_tensor("PD", [128, 512], F32))
        PE2 = est.enter_context(nc.psum_tensor("PE2", [128, 512], F32))

        lqs = [Sem(est, nc, f"lq{i}") for i in range(4)]
        sqs = [Sem(est, nc, f"sq{i}") for i in range(4)]
        hqs = [Sem(est, nc, f"hq{i}") for i in range(2)]
        LQ = DmaQ(P, "y", lqs)
        SQ = DmaQ(P, "g", sqs)
        HQ = DmaQ(P, "y", hqs)
        HS = DmaQ(P, "s", hqs[0:1])
        LQA = tuple(lqs)
        SQA = tuple(sqs)
        HQA = tuple(hqs)
        pe = Sem(est, nc, "pe")
        ve = Sem(est, nc, "ve")
        ac = Sem(est, nc, "ac")
        gp = Sem(est, nc, "gp")
        ccs = Sem(est, nc, "ccs")
        P.__init_sems__({"t": pe, "v": ve, "s": ac, "g": gp})
        P.set_dma_sems(lqs + sqs + hqs)

        def barrier():
            P.emit("t", mk_nop(), wait=[pe], inc=[pe])
            P.emit("v", mk_nop())
            P.emit("s", mk_nop())
            P.emit("g", mk_nop())
            allsems = [pe, ve, ac, gp] + list(LQA) + list(SQA) + list(HQA)
            for k in P.KEYS:
                P.emit(k, mk_nop(), wait=allsems)

        cons = AR.f32(128, 16)
        ones_c = AR.bf16(128, 2)
        ones_r = AR.f32(1, 128)
        ones_rb = AR.bf16(1, 128)
        ident_b = AR.bf16(128, 128)
        bq_s = AR.f32(128, ND)
        bk_s = AR.f32(128, ND)
        bo_s = AR.f32(128, ND)
        g2_s = AR.f32(128, ND)
        b2_s = AR.f32(128, ND)
        ffb_s = AR.f32(128, ND)
        cb_s = AR.f32(128, 4)
        gg_s = AR.f32(128, 4)
        bb_s = AR.f32(128, 4)
        lb_b = AR.bf16(1, 2 * D)
        vbb = AR.f32(128, D)
        bnacc = AR.f32(128, 8)
        bnp_sc = AR.f32(128, 4)
        bnp_bi = AR.f32(128, 4)

        CONSTS = {0: 1.0, 1: 0.125, 2: EXP_BIAS, 3: 1.0 / D, 4: -0.5, 5: EPS,
                  6: 1.0 / 32768.0, 7: 0.0, 8: -1.0}
        for col, val in CONSTS.items():
            nc.const_aps.register(F32, val, cons[:, col:col + 1])

        M_PHASE = AR.mark()
        v_sb = AR.bf16(128, NTK * H * VSLOT)
        k_sb = [AR.bf16(128, S) for _ in range(2)]
        wpq = [AR.bf16(128, ND * 128) for _ in range(2)]
        wpk = [AR.bf16(128, ND * 128) for _ in range(2)]
        wpv = [AR.bf16(128, ND * 128) for _ in range(2)]
        wtmpq = [AR.f32(128, ND * 128) for _ in range(2)]
        qstg = [AR.bf16(128, 256) for _ in range(2)]
        M_V = AR.mark()

        NW = S // 256
        pqk = PD[:, 256:512]
        pvv = PE2[:, 256:384]
        qstb = {}
        qkv_pe_done = {}
        wcast_done = {}
        wst_hist = []
        qk_evac = {}
        v_evh = {}
        hw_pe = []

        def stage_wpair(p):
            first_war = ([(pe.h, qkv_pe_done[p - 2])] if p - 2 in qkv_pe_done
                         else [])
            for mi, (wt, dstl) in enumerate(((wq, wpq), (wk, wpk), (wv, wpv))):
                i = len(wst_hist)
                wr = [wst_hist[i - 2]] if i >= 2 else []
                bw = HQ.batch(
                    [(wtmpq[i % 2].rearrange("p (k c) -> p k c", k=ND),
                      wt.rearrange("(k p) c -> p k c", k=ND)[:, :,
                                   p * 128:(p + 1) * 128])],
                    wait_raw=wr)
                for k in range(ND):
                    P.emit("v", mk_copy(dstl[p % 2][:, k * 128:(k + 1) * 128],
                                        wtmpq[i % 2][:, k * 128:(k + 1) * 128]),
                           wait=[ve],
                           wait_raw=([bw] if k == 0 else [])
                           + (first_war if (mi == 0 and k == 0) else []))
                wst_hist.append((ve.h, ve.v))
            wcast_done[p] = (ve.h, ve.v)

        def qkv_proj(p, w, hap, which, slot, chain):
            T = w * 256
            if chain:
                P.emit("t", mk_nop(), wait_raw=list(chain))
            wsrc = wpk if which == "k" else wpq
            for k in range(ND):
                P.emit("t", mk_mm(slot, wsrc[p % 2][:, k * 128:(k + 1) * 128],
                                  hap(k), k == 0, k == ND - 1),
                       inc=([pe] if k == ND - 1 else []))
            if which == "k":
                P.emit("v", mk_ts(k_sb[p % 2][:, T:T + 256], slot,
                                  bk_s[:, p:p + 1], "add"), wait=[ve],
                       wait_raw=[(pe.h, pe.v)])
            else:
                old_q = [qstb[p][w - 2]] if w >= 2 else []
                P.emit("v", mk_ts(qstg[w % 2][:, :], slot, bq_s[:, p:p + 1], "add"),
                       wait=[ve], wait_raw=[(pe.h, pe.v)] + old_q)
            P.emit("v", mk_nop())
            qk_evac[(p, which, w)] = ve.v
            if which == "q":
                qstb.setdefault(p, []).append(
                    SQ.batch([(qT_d[:, p, T:T + 256], qstg[w % 2][:, :])], wait=[ve]))

        def qkv_v_tile(p, tk, hap_t, slot, chain):
            evh = v_evh.setdefault(p, [])
            if chain:
                P.emit("t", mk_nop(), wait_raw=list(chain))
            for k in range(ND):
                P.emit("t", mk_mm(slot, hap_t(k),
                                  wpv[p % 2][:, k * 128:(k + 1) * 128],
                                  k == 0, k == ND - 1),
                       inc=([pe] if k == ND - 1 else []))
            P.emit("v", mk_nop(), wait=[ve], wait_raw=[(pe.h, pe.v)])
            for h01 in range(2):
                hidx = 2 * p + h01
                base = (tk * H + hidx) * VSLOT
                P.emit("v", mk_tt(v_sb[:, base:base + 64],
                                  slot[:, h01 * 64:(h01 + 1) * 64],
                                  vbb[:, hidx * 64:(hidx + 1) * 64], "add"))
            P.emit("v", mk_nop())
            evh.append(ve.v)

        def emitP0():
            loads = [
                (cons, ctab[:, :]), (bq_s, bq[:, :]), (bk_s, bk[:, :]),
                (bo_s, bo[:, :]), (g2_s, gam2[:, :]), (b2_s, bet2[:, :]),
                (ffb_s, ffb[:, :]),
                (cb_s[:, 0:2], cbso[:, :]), (cb_s[:, 2:4], cbsi[:, :]),
                (gg_s[:, 0:2], gso[:, :]), (gg_s[:, 2:4], gsi[:, :]),
                (bb_s[:, 0:2], bso[:, :]), (bb_s[:, 2:4], bsi[:, :]),
                (ones_r[0:1, :], onesrow[:, :]),
            ]
            tmp = AR.f32(128, 128)
            tmpr = AR.f32(1, 2 * D)
            tmpv = AR.f32(1, D)
            loads += [(tmp[:, :], identf[:, :]), (tmpr[0:1, 0:D], lbso[:, :]),
                      (tmpr[0:1, D:2 * D], lbsi[:, :]), (tmpv[0:1, :], bvr[:, :])]
            b0 = LQ.batch(loads)
            P.emit("v", mk_copy(ident_b[:, :], tmp[:, :]), wait_raw=[b0])
            P.emit("v", mk_copy(lb_b[0:1, :], tmpr[0:1, :]))
            P.emit("v", mk_copy(ones_rb[0:1, :], ones_r[0:1, :]))
            P.emit("v", mk_copy(ones_c[:, 0:1], cons[:, 0:1]))
            P.emit("v", mk_memset(bnacc[:, :], 0.0))
            for half in range(2):
                P.emit("t", mk_mm(PA[:, half * 512:(half + 1) * 512],
                                  ones_r[0:1, :], tmpv[0:1, half * 512:(half + 1) * 512],
                                  True, True), wait=[ve], wait_raw=[b0], inc=[pe])
            P.emit("v", mk_copy(vbb[:, :], PA[:, 0:1024]), wait=[pe])
            P.emit("t", mk_nop(), wait=[ve])

        emitP0()
        barrier()

        def emitP1():
            CB = 512
            NCB = S // CB
            NT = CB // 128
            stage_wpair(0)
            stage_wpair(1)
            x_c = AR.f32(128, ND * CB)
            pe_c = AR.f32(128, ND * CB)
            xbf2 = [AR.bf16(128, ND * CB) for _ in range(2)]
            st_c = AR.f32(1, 3 * CB)
            rst_c = AR.f32(1, CB)
            vtpl = AR.bf16(128, H * VSLOT)

            P.emit("v", mk_memset(vtpl[:, :], 0.0))
            P.emit("v", mk_nop(), wait=[ve])
            for hh in range(H):
                P.emit("v", mk_memset(vtpl[:, hh * VSLOT + 64:hh * VSLOT + 65], 1.0))
            P.emit("v", mk_nop(), wait=[ve])
            for tk in range(NTK):
                P.emit("v", mk_copy(v_sb[:, tk * H * VSLOT:(tk + 1) * H * VSLOT],
                                    vtpl[:, :]))
            P.emit("v", mk_nop())

            psum_a = PA[0:1, 0:CB]
            psum_b = PA[0:1, 512:512 + CB]
            pbc = PC[:, 0:2 * CB]

            def ld_p1(c):
                t0, t1 = c * CB, (c + 1) * CB
                prs = ([(x_c[:, k * CB:(k + 1) * CB], xT[:, k, t0:t1]) for k in range(ND)]
                       + [(pe_c[:, k * CB:(k + 1) * CB], pet[:, k, t0:t1]) for k in range(ND)])
                return tuple(LQ.batch(prs[i::4], wait=[ve]) for i in range(4))

            bc = ld_p1(0)
            wv_pe = {}
            bhb_hist = {}
            for c in range(NCB):
                t0, t1 = c * CB, (c + 1) * CB
                xbf = xbf2[c % 2]
                hbf = xbf
                war = ([(pe.h, wv_pe[c - 2])] if c >= 2 else []) \
                    + ([bhb_hist[c - 2]] if c >= 2 else [])
                P.emit("v", mk_nop(), wait_raw=list(bc) + war)
                for k in range(ND):
                    P.emit("v", mk_copy(xbf[:, k * CB:(k + 1) * CB], x_c[:, k * CB:(k + 1) * CB]))
                P.emit("v", mk_nop())
                P.emit("t", mk_nop(), wait=[ve])
                for k in range(ND):
                    P.emit("t", mk_mm(psum_a, ones_c[:, 0:1], xbf[:, k * CB:(k + 1) * CB],
                                      k == 0, k == ND - 1), inc=([pe] if k == ND - 1 else []))
                P.emit("v", mk_nop(), wait=[pe])
                for k in range(ND):
                    P.emit("v", mk_tt(xbf[:, k * CB:(k + 1) * CB], xbf[:, k * CB:(k + 1) * CB],
                                      xbf[:, k * CB:(k + 1) * CB], "mult"))
                P.emit("v", mk_nop())
                P.emit("t", mk_nop(), wait=[ve])
                for k in range(ND):
                    P.emit("t", mk_mm(psum_b, ones_c[:, 0:1], xbf[:, k * CB:(k + 1) * CB],
                                      k == 0, k == ND - 1), inc=([pe] if k == ND - 1 else []))
                P.emit("v", mk_ts(st_c[0:1, 0:CB], psum_a, 1.0 / D, "mult"), wait=[pe])
                P.emit("v", mk_ts(st_c[0:1, CB:2 * CB], psum_b, 1.0 / D, "mult"))
                P.emit("v", mk_tt(rst_c[0:1, 0:CB], st_c[0:1, 0:CB], st_c[0:1, 0:CB], "mult"),
                       wait=[ve])
                P.emit("v", mk_tt(st_c[0:1, 2 * CB:3 * CB], st_c[0:1, CB:2 * CB],
                                  rst_c[0:1, 0:CB], "subtract"), wait=[ve])
                P.emit("v", mk_ts(st_c[0:1, 2 * CB:3 * CB], st_c[0:1, 2 * CB:3 * CB], EPS, "add"),
                       wait=[ve])
                P.emit("s", mk_act(rst_c[0:1, 0:CB], st_c[0:1, 2 * CB:3 * CB], AF.Ln),
                       wait=[ve])
                P.emit("s", mk_act(st_c[0:1, 2 * CB:3 * CB], rst_c[0:1, 0:CB], AF.Exp, scale=-0.5),
                       wait=[ac])
                P.emit("v", mk_tt(st_c[0:1, CB:2 * CB], st_c[0:1, 0:CB],
                                  st_c[0:1, 2 * CB:3 * CB], "mult"), wait=[ac])
                P.emit("t", mk_mm(pbc[:, 0:CB], ones_r[0:1, :], st_c[0:1, 2 * CB:3 * CB], True, True),
                       wait=[ve])
                P.emit("t", mk_mm(pbc[:, CB:2 * CB], ones_r[0:1, :], st_c[0:1, CB:2 * CB], True, True),
                       inc=[pe])
                P.emit("v", mk_nop(), wait=[pe])
                for k in range(ND):
                    P.emit("v", mk_tt(x_c[:, k * CB:(k + 1) * CB], x_c[:, k * CB:(k + 1) * CB],
                                      pbc[:, 0:CB], "mult"))
                P.emit("v", mk_nop(), wait=[ve])
                for k in range(ND):
                    P.emit("v", mk_tt(x_c[:, k * CB:(k + 1) * CB], x_c[:, k * CB:(k + 1) * CB],
                                      pbc[:, CB:2 * CB], "subtract"))
                P.emit("v", mk_nop(), wait=[ve])
                for k in range(ND):
                    P.emit("v", mk_tt(x_c[:, k * CB:(k + 1) * CB], x_c[:, k * CB:(k + 1) * CB],
                                      pe_c[:, k * CB:(k + 1) * CB], "add"))
                P.emit("v", mk_nop(), wait=[ve])
                for k in range(ND):
                    P.emit("v", mk_copy(hbf[:, k * CB:(k + 1) * CB], x_c[:, k * CB:(k + 1) * CB]))
                P.emit("v", mk_nop())
                hcast_val = ve.v
                bhb = SQ.batch([(hbf_d[:, k, t0:t1], hbf[:, k * CB:(k + 1) * CB])
                                for k in range(ND)], wait=[ve])
                bhb_hist[c] = bhb
                if c + 1 < NCB:
                    ldp = []
                    for k in range(ND):
                        bsk = SQ.batch([(hT_d[:, k, t0:t1], x_c[:, k * CB:(k + 1) * CB])],
                                       wait=([ve] if k == 0 else ()))
                        ldp.append((k, bsk))
                    lb = []
                    for k, bsk in ldp:
                        lb.append(LQ.batch(
                            [(x_c[:, k * CB:(k + 1) * CB], xT[:, k, (c + 1) * CB:(c + 2) * CB]),
                             (pe_c[:, k * CB:(k + 1) * CB], pet[:, k, (c + 1) * CB:(c + 2) * CB])],
                            wait_raw=[bsk]))
                    bc = tuple(lb)
                else:
                    SQ.batch([(hT_d[:, k, t0:t1], x_c[:, k * CB:(k + 1) * CB])
                              for k in range(ND)], wait=[ve])
                P.emit("t", mk_nop(), wait_raw=[(ve.h, hcast_val)])
                for wi in range(CB // 256):
                    w = c * (CB // 256) + wi
                    hap = (lambda wi=wi: lambda k:
                           hbf[:, k * CB + wi * 256:k * CB + wi * 256 + 256])()
                    kchain = ([wcast_done[0]] if w == 0
                              else [(ve.h, qk_evac[(0, "q", w - 1)])])
                    qkv_proj(0, w, hap, "k", pqk, kchain)
                    qkv_proj(0, w, hap, "q", pqk, [(ve.h, qk_evac[(0, "k", w)])])
                    for i2 in range(2):
                        tk = 2 * w + i2
                        hapt = (lambda wi=wi, i2=i2: lambda k:
                                hbf[:, k * CB + wi * 256 + i2 * 128:
                                    k * CB + wi * 256 + (i2 + 1) * 128])()
                        vchain = ([(ve.h, v_evh[0][tk - 1])] if tk >= 1 else [])
                        qkv_v_tile(0, tk, hapt, pvv, vchain)
                wv_pe[c] = pe.v
            qkv_pe_done[0] = pe.v
            AR.reset(M_V)

        if "p01" in phases:
            emitP1()
        barrier()

        def emitP2():
            m2 = AR.mark()
            eT = [AR.bf16(128, 2 * NTK * CH) for _ in range(3)]
            q_p = [AR.bf16(128, CH) for _ in range(4)]
            ostg = [AR.bf16(128, 2 * DH) for _ in range(2)]
            rec = AR.f32(128, 2)
            hwin = [AR.bf16(128, ND * 256) for _ in range(3)]

            pgs = [PA[:, 0:1024], PB[:, 0:1024], PC[:, 0:1024]]
            pos_ = [PD[:, 0:132], PE2[:, 0:132]]
            NGRP = NTK // 4

            hwb = {}
            lwv = [None]
            def hwin_load(g):
                T = (g % NW) * 256
                wr = [(pe.h, hw_pe[g - 3])] if g >= 3 else []
                hwb[g] = HS.batch(
                    [(hwin[g % 3].rearrange("p (k t) -> p k t", k=ND),
                      hbf_d[:, :, T:T + 256])], wait_raw=wr)

            if NP > 1:
                hwin_load(0)
                if NW > 1:
                    hwin_load(1)
            exp_hist = []
            av_hist = []
            sc_hist = []
            dv_hist = []
            dm_hist = []
            bq_pend = {}

            bq_defer = []

            def issue_bq(gcc):
                jj, cc = gcc // NCH, gcc % NCH
                if len(qstb.get(jj, ())) <= cc // 2:
                    bq_defer.append(gcc)
                    return
                war = [(pe.h, sc_hist[gcc - 4])] if gcc >= 4 else []
                bq_pend[gcc] = LQ.batch(
                    [(q_p[gcc % 4][:, :], qT_d[:, jj, cc * CH:(cc + 1) * CH])],
                    wait_raw=war + [qstb[jj][cc // 2]])

            def flush_bq():
                while bq_defer:
                    jj, cc = bq_defer[0] // NCH, bq_defer[0] % NCH
                    if len(qstb.get(jj, ())) <= cc // 2:
                        break
                    issue_bq(bq_defer.pop(0))

            for gc0 in range(min(3, NP * NCH)):
                issue_bq(gc0)
            for j in range(NP):
                if j + 2 < NP:
                    stage_wpair(j + 2)
                for c in range(NCH + 1):
                    gc = j * NCH + c
                    if c < NCH:
                        t0, t1 = c * CH, (c + 1) * CH
                        kw = ([(ve.h, qk_evac[(j, "k", NW - 1)])] if c == 0 else [])
                        P.emit("t", mk_nop(), wait_raw=[bq_pend[gc]] + kw)
                        if gc + 3 < NP * NCH:
                            issue_bq(gc + 3)
                        et = eT[gc % 3]
                        for m in range(NGRP):
                            gidx = gc * NGRP + m
                            pg = pgs[gidx % 3]
                            if gidx >= 3:
                                P.emit("t", mk_nop(), wait_raw=[(ac.h, exp_hist[gidx - 3])])
                            for q4 in range(4):
                                tk = m * 4 + q4
                                P.emit("t", mk_mm(pg[:, q4 * 128:(q4 + 1) * 128],
                                                  k_sb[j % 2][0:64, tk * 128:(tk + 1) * 128],
                                                  q_p[c % 4][0:64, :], True, True, tp=(0, 0)))
                                P.emit("t", mk_mm(pg[:, 512 + q4 * 128:512 + (q4 + 1) * 128],
                                                  k_sb[j % 2][64:128, tk * 128:(tk + 1) * 128],
                                                  q_p[c % 4][64:128, :], True, True, tp=(64, 0)),
                                       inc=([pe] if q4 == 3 else []))
                            extra = [(pe.h, av_hist[gc - 3])] if (gc >= 3 and m == 0) else []
                            P.emit("s", mk_act(et[:, m * 1024:(m + 1) * 1024], pg, AF.Exp,
                                               scale=0.125, bias=EXP_BIAS),
                                   wait=[pe], wait_raw=extra)
                            exp_hist.append(ac.v)
                            if m == NGRP - 1:
                                sc_hist.append(pe.v)
                    weave = j + 1 < NP and 1 <= c <= 2 * NW
                    if weave:
                        g = j * NW + (c - 1) // 2
                        w = g % NW
                        hv = hwin[g % 3]
                        if c % 2 == 1:
                            P.emit("t", mk_nop(), wait_raw=[hwb[g]])
                        else:
                            kchain = ([wcast_done[j + 1]] if w == 0
                                      else [(ve.h, qk_evac[(j + 1, "q", w - 1)])]) \
                                + ([(ve.h, dv_hist[-1])] if dv_hist else [])
                            qkv_proj(j + 1, w,
                                     (lambda hv=hv: lambda k:
                                      hv[:, k * 256:(k + 1) * 256])(),
                                     "k", pqk, kchain)
                    if c >= 1:
                        ca = c - 1
                        ag = j * NCH + ca
                        po = pos_[ca % 2]
                        eta = eT[ag % 3]
                        if ag >= 2:
                            P.emit("t", mk_nop(), wait_raw=[(ve.h, dv_hist[ag - 2])])
                        P.emit("t", mk_nop(), wait_raw=[(ac.h, exp_hist[ag * NGRP + NGRP - 1])]
                               + ([(ve.h, lwv[0])] if lwv[0] is not None else [])
                               + ([(ve.h, v_evh[j][-1])] if ca == 0 else []))
                        for h01 in range(2):
                            hidx = 2 * j + h01
                            ocol = h01 * VSLOT
                            for tk in range(NTK):
                                ecol = (tk // 4) * 1024 + h01 * 512 + (tk % 4) * 128
                                P.emit("t", mk_mm(po[:, ocol:ocol + 65],
                                                  eta[:, ecol:ecol + 128],
                                                  v_sb[:, (tk * H + hidx) * VSLOT:
                                                       (tk * H + hidx) * VSLOT + 65],
                                                  tk == 0, tk == NTK - 1),
                                       inc=([pe] if (h01 == 1 and tk == NTK - 1) else []))
                        av_hist.append(pe.v)
                    if weave:
                        if c % 2 == 1:
                            for tk in (2 * w, 2 * w + 1):
                                vchain = ([(ve.h, v_evh[j + 1][tk - 1])]
                                          if tk >= 1 else [])
                                if tk == 2 * w and dv_hist:
                                    vchain = vchain + [(ve.h, dv_hist[-1])]
                                qkv_v_tile(j + 1, tk,
                                           (lambda hv=hv, i2=tk - 2 * w: lambda k:
                                            hv[:, k * 256 + i2 * 128:
                                                k * 256 + (i2 + 1) * 128])(),
                                           pvv, vchain)
                        else:
                            qkv_proj(j + 1, w,
                                     (lambda hv=hv: lambda k:
                                      hv[:, k * 256:(k + 1) * 256])(),
                                     "q", pqk, [(ve.h, qk_evac[(j + 1, "k", w)])])
                            hw_pe.append(pe.v)
                            if g + 2 < (NP - 1) * NW:
                                hwin_load(g + 2)
                            if c == 2 * NW:
                                qkv_pe_done[j + 1] = pe.v
                        lwv[0] = ve.v
                        flush_bq()
                    if c >= 1:
                        ost = ostg[ca % 2]
                        P.emit("v", mk_nop(), wait=[ve], wait_raw=[(pe.h, av_hist[-1])])
                        if ag >= 2:
                            P.emit("v", mk_nop(), wait_raw=[dm_hist[ag - 2]])
                        for h01 in range(2):
                            ocol = h01 * VSLOT
                            P.emit("v", mk_recip(rec[:, h01:h01 + 1], po[:, ocol + 64:ocol + 65]))
                        P.emit("v", mk_nop(), wait=[ve])
                        for h01 in range(2):
                            ocol = h01 * VSLOT
                            P.emit("v", mk_ts(ost[:, h01 * DH:(h01 + 1) * DH],
                                              po[:, ocol:ocol + 64],
                                              rec[:, h01:h01 + 1], "mult"))
                        P.emit("v", mk_nop())
                        dv_hist.append(ve.v)
                        bo_ = SQ.batch([(on_d[ca, :, 2 * j * DH:(2 * j + 2) * DH], ost[:, :])],
                                       wait=[ve])
                        dm_hist.append(bo_)
            AR.reset(m2)

        if "p2" in phases:
            emitP2()
        barrier()

        def emitP3():
            CB = 512
            NCB = S // CB
            NT = CB // 128
            ffw_s = AR.bf16(128, ND * D)
            lw_s = AR.bf16(128, 2 * 2 * D)
            m3 = AR.mark()
            wo_s = AR.bf16(128, ND * D)
            cm_s = AR.bf16(128, 2 * ND * G)
            mW = AR.mark()
            wtmps = [AR.f32(128, D) for _ in range(2)]
            jobs = ([(wo, wo_s, k * D, D) for k in range(ND)]
                    + [(ffw, ffw_s, k * D, D) for k in range(ND)]
                    + [(cmso, cm_s, k * G, G) for k in range(ND)]
                    + [(cmsi, cm_s, 2048 + k * G, G) for k in range(ND)]
                    + [(lwso, lw_s, gc * D, D) for gc in range(2)]
                    + [(lwsi, lw_s, 2048 + gc * D, D) for gc in range(2)])
            srcrow = ([(wo, k) for k in range(ND)] + [(ffw, k) for k in range(ND)]
                      + [(cmso, k) for k in range(ND)] + [(cmsi, k) for k in range(ND)]
                      + [(lwso, gc) for gc in range(2)] + [(lwsi, gc) for gc in range(2)])
            cast_hist = []
            for i, ((wt, dst, off, w), (wt2, row)) in enumerate(zip(jobs, srcrow)):
                wr = [cast_hist[i - 2]] if i >= 2 else []
                bw = LQ.batch([(wtmps[i % 2][:, 0:w], wt[row * 128:(row + 1) * 128, :])],
                              wait_raw=wr)
                P.emit("v", mk_copy(dst[:, off:off + w], wtmps[i % 2][:, 0:w]),
                       wait_raw=[bw], inc=[ve])
                cast_hist.append((ve.h, ve.v))
            AR.reset(mW)
            onrm = AR.bf16(128, NT * D)
            oT_c = AR.bf16(128, ND * CB)
            hT_c = AR.f32(128, ND * CB)
            x_c2 = AR.f32(128, ND * CB)
            y2 = [AR.f32(128, ND * CB) for _ in range(2)]
            ybf = AR.bf16(128, ND * CB)
            st2 = AR.f32(1, 3 * CB)
            rst2 = AR.f32(1, CB)
            u2 = [AR.bf16(128, ND * CB) for _ in range(2)]
            fubf = AR.bf16(128, ND * CB)
            xcsq = AR.f32(128, 4 * CB)
            xcst = AR.f32(128, 2)

            ptrs3 = [PE2[:, 0:64].bitcast(BF16), PD[:, 128:192].bitcast(BF16)]
            pp = PD[:, 0:512]
            pst2a = PB[0:1, 0:CB]
            pst2b = PB[0:1, 512:512 + CB]
            pbc2 = PC[:, 0:2 * CB]
            pcv = PA[:, 0:CB]
            pfu = PA[:, 512:512 + CB]

            def ln_stats_bcast(srcf32, wr=()):
                P.emit("s", mk_nop(), wait=[pe], wait_raw=wr)
                for n in range(ND):
                    P.emit("s", mk_act(ybf[:, n * CB:(n + 1) * CB],
                                       srcf32[:, n * CB:(n + 1) * CB], AF.Copy))
                P.emit("s", mk_nop())
                P.emit("t", mk_nop(), wait=[ac])
                for k in range(ND):
                    P.emit("t", mk_mm(pst2a, ones_c[:, 0:1], ybf[:, k * CB:(k + 1) * CB],
                                      k == 0, k == ND - 1, skip=True),
                           inc=([pe] if k == ND - 1 else []))
                P.emit("s", mk_nop(), wait=[pe])
                for n in range(ND):
                    P.emit("s", mk_act(ybf[:, n * CB:(n + 1) * CB], ybf[:, n * CB:(n + 1) * CB],
                                       AF.Square))
                P.emit("s", mk_nop())
                P.emit("t", mk_nop(), wait=[ac])
                for k in range(ND):
                    P.emit("t", mk_mm(pst2b, ones_c[:, 0:1], ybf[:, k * CB:(k + 1) * CB],
                                      k == 0, k == ND - 1, skip=True),
                           inc=([pe] if k == ND - 1 else []))
                P.emit("v", mk_ts(st2[0:1, 0:CB], pst2a, 1.0 / D, "mult"), wait=[pe])
                P.emit("v", mk_ts(st2[0:1, CB:2 * CB], pst2b, 1.0 / D, "mult"))
                P.emit("v", mk_tt(rst2[0:1, 0:CB], st2[0:1, 0:CB], st2[0:1, 0:CB], "mult"),
                       wait=[ve])
                P.emit("v", mk_tt(st2[0:1, 2 * CB:3 * CB], st2[0:1, CB:2 * CB],
                                  rst2[0:1, 0:CB], "subtract"), wait=[ve])
                P.emit("v", mk_ts(st2[0:1, 2 * CB:3 * CB], st2[0:1, 2 * CB:3 * CB], EPS, "add"),
                       wait=[ve])
                P.emit("s", mk_act(rst2[0:1, 0:CB], st2[0:1, 2 * CB:3 * CB], AF.Ln), wait=[ve])
                P.emit("s", mk_act(st2[0:1, 2 * CB:3 * CB], rst2[0:1, 0:CB], AF.Exp, scale=-0.5),
                       wait=[ac])
                P.emit("v", mk_tt(st2[0:1, CB:2 * CB], st2[0:1, 0:CB],
                                  st2[0:1, 2 * CB:3 * CB], "mult"), wait=[ac])
                P.emit("t", mk_mm(pbc2[:, 0:CB], ones_r[0:1, :], st2[0:1, 2 * CB:3 * CB],
                                  True, True), wait=[ve])
                P.emit("t", mk_mm(pbc2[:, CB:2 * CB], ones_r[0:1, :], st2[0:1, CB:2 * CB],
                                  True, True), inc=[pe])

            def ld_p3(c, w=(), wr=()):
                t0, t1 = c * CB, (c + 1) * CB
                prs = ([(onrm[:, i * D:(i + 1) * D], on_d[c * NT + i, :, :]) for i in range(NT)]
                       + [(hT_c[:, k * CB:(k + 1) * CB], hT_d[:, k, t0:t1]) for k in range(ND)]
                       + [(x_c2[:, k * CB:(k + 1) * CB], xT[:, k, t0:t1]) for k in range(ND)])
                return tuple(LQ.batch(prs[i::4], wait=w, wait_raw=wr) for i in range(4))

            bc3 = ld_p3(0, w=[pe, ve])
            sttv = {}
            bx1h = {}
            bxh = {}
            for c in range(NCB + 2):
              if c < NCB:
                t0, t1 = c * CB, (c + 1) * CB
                P.emit("t", mk_nop(), wait=[ve], wait_raw=list(bc3))
                trh = []
                for i in range(NT):
                    for k in range(ND):
                        idx = i * ND + k
                        if idx >= 2:
                            P.emit("t", mk_nop(), wait_raw=[(ve.h, trh[idx - 2])])
                        P.emit("t", mk_trans(ptrs3[idx % 2],
                                             onrm[:, i * D + k * 128:i * D + (k + 1) * 128],
                                             ident_b[:, :]), inc=[pe])
                        P.emit("v", mk_copy(oT_c[:, k * CB + i * 128:k * CB + (i + 1) * 128],
                                            ptrs3[idx % 2]), wait=[pe])
                        trh.append(ve.v)
                b_on = None
                if c + 1 < NCB:
                    prs_on = [(onrm[:, i * D:(i + 1) * D], on_d[(c + 1) * NT + i, :, :])
                              for i in range(NT)]
                    b_on = [LQ.batch(prs_on[i::2], wait_raw=[(pe.h, pe.v)])
                            for i in range(2)]
                P.emit("t", mk_nop(), wait=[ve])
                yb = y2[c % 2]
                for n in range(ND):
                    for k in range(ND):
                        P.emit("t", mk_mm(pp, wo_s[:, k * D + n * 128:k * D + (n + 1) * 128],
                                          oT_c[:, k * CB:(k + 1) * CB], k == 0, k == ND - 1),
                               inc=([pe] if k == ND - 1 else []))
                    P.emit("v", mk_stt(yb[:, n * CB:(n + 1) * CB], pp, bo_s[:, n:n + 1],
                                       hT_c[:, n * CB:(n + 1) * CB], "add", "add"), wait=[pe])
                    P.emit("t", mk_nop(), wait=[ve])
                sttv[c] = ve.v
                b_h3 = None
                if c + 1 < NCB:
                    prs_h3 = [(hT_c[:, k * CB:(k + 1) * CB],
                               hT_d[:, k, (c + 1) * CB:(c + 2) * CB])
                              for k in range(ND)]
                    b_h3 = [LQ.batch(prs_h3[i::2], wait=[ve]) for i in range(2)]
                if c + 1 < NCB:
                    bc3 = tuple(b_on) + tuple(b_h3)
              if 1 <= c <= NCB:
                cl = c - 1
                tl0, tl1 = cl * CB, (cl + 1) * CB
                yb = y2[cl % 2]
                ln_stats_bcast(yb, wr=[(ve.h, sttv[cl])])
                P.emit("v", mk_nop(), wait=[pe], wait_raw=[(ve.h, sttv[cl])]
                       + ([bxh[cl]] if cl in bxh else []))
                for n in range(ND):
                    P.emit("v", mk_tt(yb[:, n * CB:(n + 1) * CB], yb[:, n * CB:(n + 1) * CB],
                                      pbc2[:, 0:CB], "mult"))
                P.emit("v", mk_nop(), wait=[ve])
                for n in range(ND):
                    P.emit("v", mk_tt(yb[:, n * CB:(n + 1) * CB], yb[:, n * CB:(n + 1) * CB],
                                      pbc2[:, CB:2 * CB], "subtract"))
                P.emit("v", mk_nop(), wait=[ve])
                for n in range(ND):
                    P.emit("v", mk_ts2(yb[:, n * CB:(n + 1) * CB], yb[:, n * CB:(n + 1) * CB],
                                       g2_s[:, n:n + 1], "mult", b2_s[:, n:n + 1], "add"))
                P.emit("v", mk_nop(), wait=[ve])
                for n in range(ND):
                    P.emit("v", mk_tt(yb[:, n * CB:(n + 1) * CB], yb[:, n * CB:(n + 1) * CB],
                                      x_c2[:, n * CB:(n + 1) * CB], "add"))
                P.emit("v", mk_nop())
                bx1 = SQ.batch([(x1_d[:, k, tl0:tl1], yb[:, k * CB:(k + 1) * CB])
                                for k in range(ND)], wait=[ve])
                bx1h[cl] = bx1
                if cl + 1 < NCB:
                    bxh[cl + 1] = LQ.batch([(x_c2[:, k * CB:(k + 1) * CB],
                                             xT[:, k, (cl + 1) * CB:(cl + 2) * CB])
                                            for k in range(ND)], wait=[ve])
                ln_stats_bcast(yb, wr=[(ve.h, ve.v)])
                ub = u2[cl % 2]
                P.emit("v", mk_nop(), wait=[pe], wait_raw=[bx1])
                for n in range(ND):
                    P.emit("v", mk_tt(yb[:, n * CB:(n + 1) * CB], yb[:, n * CB:(n + 1) * CB],
                                      pbc2[:, 0:CB], "mult"))
                P.emit("v", mk_nop(), wait=[ve])
                for n in range(ND):
                    P.emit("v", mk_tt(ub[:, n * CB:(n + 1) * CB], yb[:, n * CB:(n + 1) * CB],
                                      pbc2[:, CB:2 * CB], "subtract"))
                P.emit("v", mk_nop())
                SQ.batch([(uT_d[:, k, tl0:tl1], ub[:, k * CB:(k + 1) * CB])
                          for k in range(ND)], wait=[ve])
              if 2 <= c:
                cf = c - 2
                tf0, tf1 = cf * CB, (cf + 1) * CB
                uf = u2[cf % 2]
                bxc = {}
                P.emit("t", mk_nop(), wait=[ve])
                for br in range(2):
                    for gc in range(2):
                        for k in range(ND):
                            P.emit("t", mk_mm(pcv, cm_s[:, br * 2048 + k * G + gc * 128:
                                                        br * 2048 + k * G + (gc + 1) * 128],
                                              uf[:, k * CB:(k + 1) * CB], k == 0, k == ND - 1),
                                   inc=([pe] if k == ND - 1 else []))
                        col = 2 * br + gc
                        P.emit("v", mk_ts(xcsq[:, col * CB:(col + 1) * CB], pcv,
                                          cb_s[:, col:col + 1], "add"), wait=[pe])
                        P.emit("t", mk_nop(), wait=[ve])
                        bxc[col] = SQ.batch([(xc_d[:, col, tf0:tf1],
                                              xcsq[:, col * CB:(col + 1) * CB])], wait=[ve])
                P.emit("v", mk_nop(), wait=[ve])
                for col in range(4):
                    P.emit("v", mk_red(xcst[:, 0:1], xcsq[:, col * CB:(col + 1) * CB], "add"))
                    P.emit("v", mk_tt(bnacc[:, col:col + 1], bnacc[:, col:col + 1],
                                      xcst[:, 0:1], "add"), wait=[ve])
                    P.emit("v", mk_tt(xcsq[:, col * CB:(col + 1) * CB],
                                      xcsq[:, col * CB:(col + 1) * CB],
                                      xcsq[:, col * CB:(col + 1) * CB], "mult"),
                           wait=[ve], wait_raw=[bxc[col]])
                    P.emit("v", mk_red(xcst[:, 1:2], xcsq[:, col * CB:(col + 1) * CB], "add"),
                           wait=[ve])
                    P.emit("v", mk_tt(bnacc[:, 4 + col:5 + col], bnacc[:, 4 + col:5 + col],
                                      xcst[:, 1:2], "add"), wait=[ve])
                P.emit("t", mk_nop(), wait=[ve])
                for n in range(ND):
                    for k in range(ND):
                        P.emit("t", mk_mm(pfu, ffw_s[:, k * D + n * 128:k * D + (n + 1) * 128],
                                          uf[:, k * CB:(k + 1) * CB], k == 0, k == ND - 1),
                               inc=([pe] if k == ND - 1 else []))
                    P.emit("v", mk_copy(fubf[:, n * CB:(n + 1) * CB], pfu), wait=[pe])
                    P.emit("t", mk_nop(), wait=[ve])
                SQ.batch([(fu_d[:, k, tf0:tf1], fubf[:, k * CB:(k + 1) * CB])
                          for k in range(ND)], wait=[ve])
            AR.reset(m3)
            return ffw_s, lw_s

        AR.reset(M_PHASE)
        if "p3" in phases:
            ffw_s, lw_s = emitP3()
        else:
            ffw_s = AR.bf16(128, ND * D); lw_s = AR.bf16(128, 2 * 2 * D)
        barrier()

        def emitCC():
            b1 = SQ.batch([(cc_i[:, :], bnacc[:, :])])
            P.emit("g", mk_nop(), wait_raw=[b1])
            P.emit("g", mk_coll(cc_i[:, :], cc_o[:, :], ccs.h), noauto=True)
            ccs.v = 1
            P.emit("g", mk_nop(), wait_raw=[(ccs.h, 1)])
            b2 = SQ.batch([(bnacc[:, :], cc_o[:, :])])
            P.emit("v", mk_nop(), wait_raw=[b2])
            tmp = AR.f32(128, 4)
            for col in range(4):
                P.emit("v", mk_ts(tmp[:, 0:1], bnacc[:, col:col + 1], 1.0 / 32768.0, "mult"))
                P.emit("v", mk_ts(tmp[:, 1:2], bnacc[:, 4 + col:5 + col], 1.0 / 32768.0, "mult"),
                       wait=[ve])
                P.emit("v", mk_tt(tmp[:, 2:3], tmp[:, 0:1], tmp[:, 0:1], "mult"),
                       wait=[ve])
                P.emit("v", mk_tt(tmp[:, 1:2], tmp[:, 1:2], tmp[:, 2:3], "subtract"),
                       wait=[ve])
                P.emit("v", mk_ts(tmp[:, 1:2], tmp[:, 1:2], EPS, "add"), wait=[ve])
                P.emit("s", mk_act(tmp[:, 2:3], tmp[:, 1:2], AF.Ln), wait=[ve])
                P.emit("s", mk_act(tmp[:, 3:4], tmp[:, 2:3], AF.Exp, scale=-0.5),
                       wait=[ac])
                P.emit("v", mk_tt(bnp_sc[:, col:col + 1], tmp[:, 3:4], gg_s[:, col:col + 1],
                                  "mult"), wait=[ac])
                P.emit("v", mk_tt(tmp[:, 0:1], tmp[:, 0:1], bnp_sc[:, col:col + 1], "mult"),
                       wait=[ve])
                P.emit("v", mk_tt(bnp_bi[:, col:col + 1], bb_s[:, col:col + 1], tmp[:, 0:1],
                                  "subtract"), wait=[ve])

        if "cc" in phases and "fb" not in phases:
            emitCC()
            barrier()

        def emitFB():
            CB = 512
            NCB = S // CB
            NT = CB // 128
            NTL = 2 * NT
            m4 = AR.mark()
            xc_c = AR.f32(128, 4 * CB)
            fu_c = AR.bf16(128, ND * CB)
            u_c = AR.bf16(128, ND * CB)
            x1_c = AR.f32(128, ND * CB)
            xa = AR.bf16(128, 4 * CB)
            e_s = [AR.f32(128, D) for _ in range(NTL)]
            z0 = [AR.f32(128, D) for _ in range(NTL)]
            m8 = AR.f32(128, 8 * NTL)
            den = AR.f32(128, 2 * NTL)
            gbf = [AR.bf16(128, D) for _ in range(2)]
            gT = [AR.bf16(128, ND * CB) for _ in range(2)]
            ug = AR.bf16(128, ND * CB)
            tmb = AR.f32(128, ND * CB)
            yo = tmb

            pLs = [PA[:, 0:1024], PC[:, 0:1024]]
            ptrs = [PE2[:, 0:64].bitcast(BF16), PD[:, 0:64].bitcast(BF16)]
            pI = [PB[:, 0:CB], PB[:, 512:512 + CB]]

            def ld_fb(tn, w=(), wr=()):
                prs = ([(xc_c.rearrange("p (q t) -> p q t", q=4), xc_d[:, :, tn:tn + CB])]
                       + [(fu_c[:, k * CB:(k + 1) * CB], fu_d[:, k, tn:tn + CB])
                          for k in range(ND)]
                       + [(u_c[:, k * CB:(k + 1) * CB], uT_d[:, k, tn:tn + CB])
                          for k in range(ND)])
                return [LQ.batch(prs[i::4], wait=w, wait_raw=wr) for i in range(4)]

            bfb = ld_fb(0, w=[ve])
            bx1b = [LQ.batch([(x1_c[:, k * CB:(k + 1) * CB], x1_d[:, k, 0:CB])
                              for k in range(ND)][i::2], wait=[ve]) for i in range(2)]
            if "cc" in phases:
                emitCC()
            barrier()
            exp_done = []
            sub_done = {}
            xa_free = {}
            gelu_ac = {}
            for c in range(NCB):
                t0, t1 = c * CB, (c + 1) * CB
                P.emit("s", mk_nop(),
                       wait_raw=list(bfb) + ([(pe.h, xa_free[c - 1])] if c else []))
                for col in range(4):
                    P.emit("s", mk_act(xa[:, col * CB:(col + 1) * CB],
                                       xc_c[:, col * CB:(col + 1) * CB], AF.Gelu,
                                       scale=bnp_sc[:, col:col + 1],
                                       bias=bnp_bi[:, col:col + 1]))
                P.emit("s", mk_nop())
                gelu_ac[c] = ac.v
                P.emit("t", mk_nop(), wait_raw=[(ac.h, gelu_ac[c])])
                for ti in range(NTL):
                    br, tt = ti // NT, ti % NT
                    pL = pLs[ti % 2]
                    gti = c * NTL + ti
                    if gti >= 2:
                        P.emit("t", mk_nop(), wait_raw=[(ac.h, exp_done[gti - 2])])
                    for half in range(2):
                        for gc in range(2):
                            col = 2 * br + gc
                            P.emit("t", mk_mm(pL[:, half * 512:(half + 1) * 512],
                                              xa[:, col * CB + tt * 128:col * CB + (tt + 1) * 128],
                                              lw_s[:, br * 2048 + gc * D + half * 512:
                                                   br * 2048 + gc * D + (half + 1) * 512],
                                              gc == 0, False))
                        P.emit("t", mk_mm(pL[:, half * 512:(half + 1) * 512],
                                          ones_rb[0:1, :],
                                          lb_b[0:1, br * D + half * 512:br * D + (half + 1) * 512],
                                          False, True), inc=([pe] if half == 1 else []))
                    if ti == NTL - 1:
                        xa_free[c] = pe.v
                    P.emit("s", mk_act(e_s[ti][:, :], pL, AF.Exp), wait=[pe],
                           wait_raw=([(ve.h, sub_done[(c - 1, ti)])] if c else []))
                    exp_done.append(ac.v)
                P.emit("v", mk_nop(), wait=[ac])
                for ti in range(NTL):
                    P.emit("v", mk_red(den[:, 2 * ti:2 * ti + 1], e_s[ti][:, :], "add"))
                P.emit("v", mk_nop(), wait=[ve])
                for ti in range(NTL):
                    P.emit("v", mk_recip(den[:, 2 * ti + 1:2 * ti + 2],
                                         den[:, 2 * ti:2 * ti + 1]))
                for r in range(4):
                    P.emit("v", mk_nop(), wait=[ve])
                    for ti in range(NTL):
                        srcz = e_s[ti] if r == 0 else z0[ti]
                        P.emit("v", mk_max8(m8[:, 8 * ti:8 * (ti + 1)], srcz[:, :]))
                    P.emit("v", mk_nop(), wait=[ve])
                    for ti in range(NTL):
                        srcz = e_s[ti] if r == 0 else z0[ti]
                        P.emit("v", mk_mr(z0[ti][:, :], m8[:, 8 * ti:8 * (ti + 1)],
                                          srcz[:, :], 0.0))
                P.emit("v", mk_nop(), wait=[ve])
                tr_hist = []
                for ti in range(NTL):
                    br, tt = ti // NT, ti % NT
                    P.emit("v", mk_tt(e_s[ti][:, :], e_s[ti][:, :], z0[ti][:, :], "subtract"),
                           wait=([ve] if ti == 0 else ()))
                    gb = gbf[ti % 2]
                    if ti >= 2:
                        P.emit("v", mk_nop(), wait_raw=[(pe.h, tr_hist[(ti - 2) * ND + ND - 1][0])])
                    P.emit("v", mk_ts(gb[:, :], e_s[ti][:, :],
                                      den[:, 2 * ti + 1:2 * ti + 2], "mult"), wait=[ve])
                    sub_done[(c, ti)] = ve.v
                    for k in range(ND):
                        i = ti * ND + k
                        if i >= 2:
                            P.emit("t", mk_nop(), wait_raw=[(ac.h, tr_hist[i - 2][1])])
                        P.emit("t", mk_trans(ptrs[i % 2], gb[:, k * 128:(k + 1) * 128],
                                             ident_b[:, :]),
                               wait=([ve] if k == 0 else ()), inc=[pe])
                        P.emit("s", mk_act(gT[br][:, k * CB + tt * 128:k * CB + (tt + 1) * 128],
                                           ptrs[i % 2], AF.Copy), wait=[pe])
                        tr_hist.append((pe.v, ac.v))
                P.emit("v", mk_nop(), wait=[ve, pe, ac])
                for k in range(ND):
                    P.emit("v", mk_tt(ug[:, k * CB:(k + 1) * CB], u_c[:, k * CB:(k + 1) * CB],
                                      gT[1][:, k * CB:(k + 1) * CB], "mult"))
                P.emit("v", mk_nop())
                for n in range(ND):
                    P.emit("v", mk_tt(tmb[:, n * CB:(n + 1) * CB], fu_c[:, n * CB:(n + 1) * CB],
                                      gT[0][:, n * CB:(n + 1) * CB], "mult"),
                           wait=([ve] if n == 0 else ()))
                P.emit("v", mk_nop())
                if c + 1 < NCB:
                    bfb = ld_fb((c + 1) * CB, w=[ve])
                P.emit("t", mk_nop(), wait=[ve])
                for n in range(ND):
                    pi = pI[n % 2]
                    for k in range(ND):
                        P.emit("t", mk_mm(pi, ffw_s[:, k * D + n * 128:k * D + (n + 1) * 128],
                                          ug[:, k * CB:(k + 1) * CB], k == 0, k == ND - 1),
                               inc=([pe] if k == ND - 1 else []))
                    P.emit("v", mk_tt(tmb[:, n * CB:(n + 1) * CB], tmb[:, n * CB:(n + 1) * CB],
                                      pi, "add"), wait=[pe, ve])
                    P.emit("t", mk_nop(), wait=[ve])
                    P.emit("v", mk_stt(yo[:, n * CB:(n + 1) * CB], tmb[:, n * CB:(n + 1) * CB],
                                       ffb_s[:, n:n + 1], x1_c[:, n * CB:(n + 1) * CB],
                                       "add", "add"), wait=[ve],
                           wait_raw=(list(bx1b) if n == 0 else ()))
                P.emit("g", mk_nop(), wait=[ve])
                SQ.batch([(yout[:, k, t0:t1], yo[:, k * CB:(k + 1) * CB])
                          for k in range(ND)], wait=[ve])
                if c + 1 < NCB:
                    tn = (c + 1) * CB
                    bx1b = [LQ.batch([(x1_c[:, k * CB:(k + 1) * CB], x1_d[:, k, tn:tn + CB])
                                      for k in range(ND)][i::2], wait=[ve])
                            for i in range(2)]
                P.emit("v", mk_nop(), wait=SQA)
            AR.reset(m4)

        if "fb" in phases:
            emitFB()
        barrier()
        P.emit("g", mk_nop(), wait=SQA + LQA)

        with nc.Block() as block:
            block.tensor(lambda T: [f(T) for f in P.ops["t"]])
            block.vector(lambda V: [f(V) for f in P.ops["v"]])
            block.scalar(lambda A: [f(A) for f in P.ops["s"]])
            block.gpsimd(lambda GP: [f(GP) for f in P.ops["g"]])
            block.sync(lambda SY: [f(SY) for f in P.ops["y"]])
    return nc



def _pe_table():
    pos = np.arange(S, dtype=np.float64)[:, None]
    div = np.exp(np.arange(0, D, 2, dtype=np.float64) * (-math.log(10000.0) / D))
    pe_t = np.zeros((S, D), dtype=np.float32)
    pe_t[:, 0::2] = np.sin(pos * div).astype(np.float32)
    pe_t[:, 1::2] = np.cos(pos * div).astype(np.float32)
    return pe_t


def _feat_major(a2d):
    return np.ascontiguousarray(a2d.T.reshape(ND, 128, S).transpose(1, 0, 2))


def _col_major_bias(b):
    return np.ascontiguousarray(b.reshape(ND, 128).T)


def _embed_conv(cw):
    cm = np.zeros((D, G), dtype=np.float32)
    g = np.arange(G)
    for f in range(4):
        cm[g * 4 + f, g] = cw[:, f]
    return cm


def kernel(**inp):
    x = np.asarray(inp["L0_attn_x"], dtype=np.float32)
    pe_t = _pe_table()

    def f32(v):
        return np.ascontiguousarray(np.asarray(v, dtype=np.float32))

    ctab = np.zeros((128, 16), np.float32)
    for col, val in {0: 1.0, 1: 0.125, 2: EXP_BIAS, 3: 1.0 / D, 4: -0.5, 5: EPS,
                     6: 1.0 / 32768.0, 7: 0.0, 8: -1.0}.items():
        ctab[:, col] = val

    shared = {
        "pet": _feat_major(pe_t),
        "wq": f32(inp["L0_attn_Wq"]), "wk": f32(inp["L0_attn_Wk"]),
        "wv": f32(inp["L0_attn_Wv"]), "wo": f32(inp["L0_attn_Wo"]),
        "bq": _col_major_bias(f32(inp["L0_attn_bq"])),
        "bk": _col_major_bias(f32(inp["L0_attn_bk"])),
        "bvr": f32(inp["L0_attn_bv"])[None, :],
        "bo": _col_major_bias(f32(inp["L0_attn_bo"])),
        "gam2": _col_major_bias(f32(inp["L0_attn_gamma"])),
        "bet2": _col_major_bias(f32(inp["L0_attn_beta"])),
        "ffw": f32(inp["L0_ffn_weight"]),
        "ffb": _col_major_bias(f32(inp["L0_ffn_bias"])),
        "cmso": _embed_conv(f32(inp["L0_ffn_so_conv_w"])),
        "cmsi": _embed_conv(f32(inp["L0_ffn_si_conv_w"])),
        "cbso": np.ascontiguousarray(f32(inp["L0_ffn_so_conv_b"]).reshape(2, 128).T),
        "cbsi": np.ascontiguousarray(f32(inp["L0_ffn_si_conv_b"]).reshape(2, 128).T),
        "gso": np.ascontiguousarray(f32(inp["L0_ffn_so_gamma"]).reshape(2, 128).T),
        "bso": np.ascontiguousarray(f32(inp["L0_ffn_so_beta"]).reshape(2, 128).T),
        "gsi": np.ascontiguousarray(f32(inp["L0_ffn_si_gamma"]).reshape(2, 128).T),
        "bsi": np.ascontiguousarray(f32(inp["L0_ffn_si_beta"]).reshape(2, 128).T),
        "lwso": np.ascontiguousarray(f32(inp["L0_ffn_so_lin_w"]).T),
        "lwsi": np.ascontiguousarray(f32(inp["L0_ffn_si_lin_w"]).T),
        "lbso": f32(inp["L0_ffn_so_lin_b"])[None, :],
        "lbsi": f32(inp["L0_ffn_si_lin_b"])[None, :],
        "identf": np.eye(128, dtype=np.float32),
        "ctab": ctab,
        "onesrow": np.ones((1, 128), np.float32),
    }

    nc = build_nc()
    in_maps = []
    for i in range(NCORES):
        m = dict(shared)
        m["xT"] = _feat_major(x[i])
        in_maps.append(m)
    res = run_bass_kernel_spmd(nc, in_maps, core_ids=list(range(NCORES)))
    outs = []
    for i in range(NCORES):
        y = np.asarray(res.results[i]["yout"])
        outs.append(y.transpose(1, 0, 2).reshape(D, S).T)
    return np.stack(outs, axis=0)


if __name__ == "__main__":
    rng = np.random.default_rng(0)
    print("building program ...")
    nc = build_nc()
    print("built")
